# version 07
# baseline (speedup 1.0000x reference)
import numpy as np
import ml_dtypes
from contextlib import ExitStack

import concourse.bass as bass
import concourse.mybir as mybir
from concourse.bass_utils import run_bass_kernel_spmd
from concourse.alu_op_type import AluOpType as Op

F32 = mybir.dt.float32
BF16 = mybir.dt.bfloat16
I8 = mybir.dt.int8
AF = mybir.ActivationFunctionType
AX = mybir.AxisListType

TOK = 2048
D = 1024
IFF = 2816
NH, DKH, DVH = 4, 256, 512
KD, VD = 1024, 2048
CHK = 256
NCH = TOK // CHK
NTT = TOK // 128
NKD = D // 128
NIC = IFF // 128
EPS = 1e-5
QMAX = 127.0

STAGE = 3
PARANOID = False
DEBUG = False
DEBUG2 = False

ENGINES = ("sync", "vector", "scalar", "tensor", "gpsimd")


class _Rec:
    __slots__ = ("eng", "fn", "kind", "need", "val", "ch", "seq")

    def __init__(self, eng, fn, kind, ch=None, seq=0):
        self.eng, self.fn, self.kind = eng, fn, kind
        self.need, self.val = False, None
        self.ch = ch
        self.seq = seq


class Sched:
    def __init__(self):
        self.items = {e: [] for e in ENGINES}
        self.cc_recs = []
        self.last = {e: None for e in ENGINES}
        self.chlast = {}

    def op(self, eng, fname, **kw):
        def fn(e, fname=fname, kw=kw):
            return getattr(e, fname)(**kw)
        r = _Rec(eng, fn, "c")
        self.items[eng].append(r)
        self.last[eng] = r
        return r

    def barrier(self):
        lasts = list(self.last.items()) + list(self.chlast.items())
        for e in ENGINES:
            for o, r in lasts:
                if r is not None and not (o == e and r.kind == "c"):
                    self.wait(e, r)

    def dma(self, eng, ch, out, in_, **kw):
        def fn(e, out=out, in_=in_, kw=kw):
            return e.dma_start(out=out, in_=in_, **kw)
        seq = self.chlast[ch].seq + 1 if ch in self.chlast else 1
        r = _Rec(eng, fn, "d", ch, seq)
        self.items[eng].append(r)
        self.chlast[ch] = r
        return r

    def dma_tr(self, eng, ch, out, in_):
        def fn(e, out=out, in_=in_):
            return e.dma_start_transpose(out=out, in_=in_)
        seq = self.chlast[ch].seq + 1 if ch in self.chlast else 1
        r = _Rec(eng, fn, "d", ch, seq)
        self.items[eng].append(r)
        self.chlast[ch] = r
        return r

    def cc(self, fn):
        r = _Rec("gpsimd", fn, "cc")
        self.items["gpsimd"].append(r)
        self.cc_recs.append(r)
        self.last["gpsimd"] = r
        return r

    def wait(self, eng, *recs):
        bych = {}
        rest = []
        for r in recs:
            if r is None:
                continue
            if r.kind == "d":
                k = r.ch
                if k not in bych or bych[k].seq < r.seq:
                    bych[k] = r
            else:
                rest.append(r)
        for r in list(bych.values()) + rest:
            r.need = True
            self.items[eng].append(("w", r))

    def finalize(self, nc, block, stack):
        sems = {e: stack.enter_context(nc.semaphore(f"se_{e}")) for e in ENGINES}
        chs = sorted(self.chlast)
        chsems = {ch: stack.enter_context(nc.semaphore(f"sc_{ch}")) for ch in chs}
        cc_sems = [stack.enter_context(nc.semaphore(f"scc{i}"))
                   for i in range(len(self.cc_recs))]
        cc_map = {id(r): s for r, s in zip(self.cc_recs, cc_sems)}
        chcnt = {ch: 0 for ch in chs}
        for e in ENGINES:
            c = 0
            for it in self.items[e]:
                if not isinstance(it, _Rec):
                    continue
                if it.kind == "d":
                    it.need = True
                    chcnt[it.ch] += 16
                    it.val = chcnt[it.ch]
                elif it.kind == "c" and it.need:
                    c += 1
                    it.val = c

        def mk(e):
            def emit(engine_obj):
                for it in self.items[e]:
                    if isinstance(it, _Rec):
                        inst = it.fn(engine_obj)
                        if it.need:
                            if it.kind == "cc":
                                inst.then_inc(cc_map[id(it)])
                            elif it.kind == "d":
                                inst.then_inc(chsems[it.ch], 16)
                            else:
                                inst.then_inc(sems[e], 1)
                    else:
                        _, r = it
                        if r.kind == "cc":
                            engine_obj.wait_ge(cc_map[id(r)], 1)
                        elif r.kind == "d":
                            engine_obj.wait_ge(chsems[r.ch], r.val)
                        else:
                            engine_obj.wait_ge(sems[r.eng], r.val)
            return emit

        block.sync(mk("sync"))
        block.vector(mk("vector"))
        block.scalar(mk("scalar"))
        block.tensor(mk("tensor"))
        block.gpsimd(mk("gpsimd"))


def build_program():
    nc = bass.Bass("TRN2", target_bir_lowering=False, debug=False)

    xs = nc.dram_tensor("xs", [TOK, D], F32, kind="ExternalInput").ap()
    Wq = nc.dram_tensor("Wq", [D, KD], F32, kind="ExternalInput").ap()
    Wk = nc.dram_tensor("Wk", [D, KD], F32, kind="ExternalInput").ap()
    Wv = nc.dram_tensor("Wv", [D, VD], F32, kind="ExternalInput").ap()
    Wg = nc.dram_tensor("Wg", [D, VD], F32, kind="ExternalInput").ap()
    gk1 = nc.dram_tensor("gk1", [D, 16], F32, kind="ExternalInput").ap()
    gk2 = nc.dram_tensor("gk2", [16, KD], F32, kind="ExternalInput").ap()
    gkb = nc.dram_tensor("gkb", [KD], F32, kind="ExternalInput").ap()
    gnw = nc.dram_tensor("gnw", [DVH], F32, kind="ExternalInput").ap()
    Wo = nc.dram_tensor("Wo", [VD, D], F32, kind="ExternalInput").ap()
    clipx = nc.dram_tensor("clipx", [1], F32, kind="ExternalInput").ap()
    clipd = nc.dram_tensor("clipd", [1], F32, kind="ExternalInput").ap()
    wga = nc.dram_tensor("wga", [IFF, D], F32, kind="ExternalInput").ap()
    wup = nc.dram_tensor("wup", [IFF, D], F32, kind="ExternalInput").ap()
    wdo = nc.dram_tensor("wdo", [D, IFF], F32, kind="ExternalInput").ap()
    gmask = nc.dram_tensor("gmask", [4], F32, kind="ExternalInput").ap()
    out = nc.dram_tensor("out", [TOK, D], F32, kind="ExternalOutput").ap()
    if DEBUG:
        d_xq = nc.dram_tensor("d_xq", [TOK, D], F32, kind="ExternalOutput").ap()
        d_h = nc.dram_tensor("d_h", [256, TOK], F32, kind="ExternalOutput").ap()
        d_wg = nc.dram_tensor("d_wg", [256, D], F32, kind="ExternalOutput").ap()
        d_cols = nc.dram_tensor("d_cols", [128, 168], F32, kind="ExternalOutput").ap()
        d_row = nc.dram_tensor("d_row", [1, 512], F32, kind="ExternalOutput").ap()
    if DEBUG2:
        d_q = nc.dram_tensor("d_q", [KD, TOK], F32, kind="ExternalOutput").ap()
        d_k = nc.dram_tensor("d_k", [KD, TOK], F32, kind="ExternalOutput").ap()
        d_g = nc.dram_tensor("d_g", [KD, TOK], F32, kind="ExternalOutput").ap()
        d_v = nc.dram_tensor("d_v", [TOK, VD], F32, kind="ExternalOutput").ap()
        d_sg = nc.dram_tensor("d_sg", [TOK, VD], F32, kind="ExternalOutput").ap()
        d_ol = nc.dram_tensor("d_ol", [TOK, VD], F32, kind="ExternalOutput").ap()
        d_qg = nc.dram_tensor("d_qg", [KD, TOK], F32, kind="ExternalOutput").ap()
        d_og = nc.dram_tensor("d_og", [TOK, VD], F32, kind="ExternalOutput").ap()
        d_xn1 = nc.dram_tensor("d_xn1", [TOK, D], F32, kind="ExternalOutput").ap()
        d_ccin = nc.dram_tensor("d_ccin", [KD, 513], F32, kind="ExternalOutput").ap()
        d_ccout = nc.dram_tensor("d_ccout", [4 * KD, 513], F32, kind="ExternalOutput").ap()

    wdg_bf = nc.dram_tensor("wdg_bf", [IFF, D], BF16).ap()
    wdu_bf = nc.dram_tensor("wdu_bf", [IFF, D], BF16).ap()
    wdd_bf = nc.dram_tensor("wdd_bf", [D, IFF], BF16).ap()
    xq_d = nc.dram_tensor("xq_d", [TOK, D], BF16).ap()
    h_d = nc.dram_tensor("h_d", [IFF, TOK], BF16).ap()
    xn1_d = nc.dram_tensor("xn1_d", [TOK, D], BF16).ap()
    x1_d = nc.dram_tensor("x1_d", [TOK, D], F32).ap()
    qg_d = nc.dram_tensor("qg_d", [KD, TOK], BF16).ap()
    q_d = nc.dram_tensor("q_d", [KD, TOK], BF16).ap()
    k_d = nc.dram_tensor("k_d", [KD, TOK], BF16).ap()
    g_d = nc.dram_tensor("g_d", [KD, TOK], BF16).ap()
    v_d = nc.dram_tensor("v_d", [TOK, VD], BF16).ap()
    sg_d = nc.dram_tensor("sg_d", [TOK, VD], BF16).ap()
    ol_d = nc.dram_tensor("ol_d", [TOK, VD], BF16).ap()
    og_d = nc.dram_tensor("og_d", [TOK, VD], BF16).ap()
    kh_d = nc.dram_tensor("kh_d", [KD, CHK], BF16).ap()
    cc_sin = nc.dram_tensor("cc_sin", [KD + 128, 512], F32)
    cc_sout = nc.dram_tensor("cc_sout", [4 * (KD + 128), 512], F32)
    mx_in = [nc.dram_tensor(f"mx_in{i}", [1, 8], F32) for i in range(4)]
    mx_out = [nc.dram_tensor(f"mx_out{i}", [1, 8], F32) for i in range(4)]

    S = Sched()

    with ExitStack() as stk:
        def sb(name, shape, dt):
            return stk.enter_context(nc.sbuf_tensor(name, shape, dt))

        def psb(ps, name, shape, dt):
            return ps.enter_context(nc.sbuf_tensor(name, shape, dt))

        cmX1 = nc.sbuf_tensor("A_x1", [128, NTT * D], F32, side="right")
        A_x = cmX1.__enter__()
        A_cols = sb("A_cols", [128, 168], F32)
        A_row = sb("A_row", [1, 512], F32)
        A_i8 = sb("A_i8", [128, IFF], I8)
        A_const = sb("A_const", [128, 2], F32)
        nc.const_aps.register(F32, 1.0, A_const[:, 0:1])
        nc.const_aps.register(F32, EPS, A_const[:, 1:2])
        PS = [stk.enter_context(nc.psum_tensor(f"ps{i}", [128, 512], F32))
              for i in range(7)]
        PBT = stk.enter_context(nc.psum_tensor("pbt", [128, 1024], BF16))
        block = stk.enter_context(nc.Block())

        CM_M, CM_R, CM_XMX, CM_XMN, CM_AM = 0, 16, 32, 48, 64
        CM_HMX, CM_SX, CM_ISX, CM_SH, CM_ISH = 80, 81, 82, 83, 84
        CM_T0 = 85
        CM_GM = 105
        CM_RMS = 109
        CM_GNW = 113
        CM_DAC = 117
        CM_EBC = 125
        CM_EC = 133
        CM_EM = 141
        CM_NM = 149
        CM_PM = 157

        col = lambda i: A_cols[:, i:i + 1]
        cols = lambda i, n: A_cols[:, i:i + n]

        cst1 = S.op("vector", "memset", ap=A_const[:, 0:1], constant=1.0)
        cst2 = S.op("vector", "memset", ap=A_const[:, 1:2], constant=EPS)
        S.wait("scalar", cst1, cst2)
        dz = sb("dz", [128, 512], F32)
        zdz = S.op("vector", "memset", ap=dz[:, :], constant=0.0)
        S.wait("sync", zdz)
        wz = [S.dma("sync", "warm", out=mx_in[0].ap(), in_=dz[0:1, 0:8])]
        for rr in range(9):
            wz.append(S.dma("sync", "warm",
                            out=cc_sin.ap()[rr * 128:(rr + 1) * 128, :],
                            in_=dz[:, :]))
        S.wait("gpsimd", *wz)
        war = S.cc(lambda g: g.collective_compute(
            "AllReduce", Op.max, replica_groups=[list(range(8))],
            ins=[mx_in[0].ap().opt()], outs=[mx_out[0].ap().opt()]))
        wag = S.cc(lambda g: g.collective_compute(
            "AllGather", Op.bypass, replica_groups=[[0, 1, 2, 3], [4, 5, 6, 7]],
            ins=[cc_sin.ap().opt()], outs=[cc_sout.ap().opt()]))
        ld_x = [S.dma("sync", "ldx", out=A_x[:, t * D:(t + 1) * D],
                      in_=xs[t * 128:(t + 1) * 128, :]) for t in range(NTT)]
        ld_clip = S.dma("sync", "ldc", out=A_row[0:1, 0:1], in_=clipx[None, :])
        ld_clipd = S.dma("sync", "ldc", out=A_row[0:1, 1:2], in_=clipd[None, :])
        ld_gm = S.dma("sync", "ldc", out=A_row[0:1, 8:12], in_=gmask[None, :])

        def emit_weight_quant(ps):
            sp_gu, sp_d = [], []
            with ExitStack() as wps:
                NWB = 3
                wt = [psb(wps, f"wqt{i}", [128, D], F32) for i in range(NWB)]
                wb = [psb(wps, f"wqb{i}", [128, D], BF16) for i in range(NWB)]
                hist = []
                for src, dst in ((wga, wdg_bf), (wup, wdu_bf)):
                    for tt in range(IFF // 128):
                        b = len(hist) % NWB
                        tc = CM_T0 + 3 * (len(hist) % 2)
                        if len(hist) >= NWB:
                            S.wait("sync", hist[-NWB]["st"])
                            S.wait("scalar", hist[-NWB]["st"])
                        if len(hist) >= 2:
                            S.wait("vector", hist[-2]["a2"])
                        ld = S.dma("sync", f"wq{b}", out=wt[b][:, :],
                                   in_=src[tt * 128:(tt + 1) * 128, :])
                        S.wait("vector", ld)
                        rm = S.op("vector", "tensor_reduce", out=col(tc),
                                  in_=wt[b][:, :], axis=AX.X, op=Op.max,
                                  apply_absolute_value=True)
                        S.wait("vector", rm)
                        sc = S.op("vector", "tensor_scalar", out=col(tc + 1),
                                  in0=col(tc), scalar1=1.0 / QMAX, scalar2=None,
                                  op0=Op.mult)
                        S.wait("vector", sc)
                        iv = S.op("vector", "reciprocal", out=col(tc + 2),
                                  in_=col(tc + 1))
                        S.wait("scalar", iv, sc)
                        a1 = S.op("scalar", "activation", out=A_i8[:, 0:D],
                                  in_=wt[b][:, :], func=AF.Copy, scale=col(tc + 2))
                        S.wait("scalar", a1)
                        a2 = S.op("scalar", "activation", out=wb[b][:, :],
                                  in_=A_i8[:, 0:D], func=AF.Copy, scale=col(tc + 1))
                        S.wait("sync", a2)
                        st = S.dma("sync", f"ws{len(hist) % NWB}", out=dst[tt * 128:(tt + 1) * 128, :],
                                   in_=wb[b][:, :])
                        hist.append(dict(st=st, a2=a2))
                        sp_gu.append(st)
                guard = list(hist[-3:])
            with ExitStack() as wps:
                wt = psb(wps, "wqtd", [128, IFF], F32)
                wb = psb(wps, "wqbd", [128, IFF], BF16)
                prev = None
                for tt in range(D // 128):
                    tc = CM_T0
                    if prev is not None:
                        S.wait("sync", prev["st"])
                        S.wait("vector", prev["a2"])
                        S.wait("scalar", prev["st"])
                    else:
                        S.wait("sync", guard[-1]["a2"], guard[-1]["st"],
                               guard[-2]["st"], guard[-3]["st"])
                        S.wait("scalar", guard[-1]["st"], guard[-2]["st"],
                               guard[-3]["st"])
                        S.wait("vector", guard[-1]["a2"])
                    ld = S.dma("sync", "wq0", out=wt[:, :],
                               in_=wdo[tt * 128:(tt + 1) * 128, :])
                    S.wait("vector", ld)
                    rm = S.op("vector", "tensor_reduce", out=col(tc), in_=wt[:, :],
                              axis=AX.X, op=Op.max, apply_absolute_value=True)
                    S.wait("vector", rm)
                    ts1 = S.op("vector", "tensor_scalar", out=col(tc + 1),
                               in0=col(tc), scalar1=1.0 / QMAX, scalar2=None,
                               op0=Op.mult)
                    S.wait("vector", ts1)
                    iv = S.op("vector", "reciprocal", out=col(tc + 2),
                              in_=col(tc + 1))
                    sc2 = S.op("vector", "tensor_scalar", out=col(tc + 1),
                               in0=col(tc + 1), scalar1=0.5, scalar2=None,
                               op0=Op.mult)
                    S.wait("scalar", iv, sc2)
                    a1 = S.op("scalar", "activation", out=A_i8[:, 0:IFF],
                              in_=wt[:, :], func=AF.Copy, scale=col(tc + 2))
                    S.wait("scalar", a1)
                    a2 = S.op("scalar", "activation", out=wb[:, :],
                              in_=A_i8[:, 0:IFF], func=AF.Copy, scale=col(tc + 1))
                    S.wait("sync", a2)
                    st = S.dma("sync", "ws0", out=wdd_bf[tt * 128:(tt + 1) * 128, :],
                               in_=wb[:, :])
                    prev = dict(st=st, a2=a2)
                    sp_d.append(st)
            return sp_gu, sp_d

        def emit_ln_stats(scr, want_minmax):
            recs = []
            vv_hist = [None, None]
            for t in range(NTT):
                pa, pb, pc = CM_T0 + 6 * (t % 2), CM_T0 + 6 * (t % 2) + 1, \
                    CM_T0 + 6 * (t % 2) + 2
                xt = A_x[:, t * D:(t + 1) * D]
                S.wait("scalar", vv_hist[t % 2])
                s1 = S.op("scalar", "activation", out=scr[:, :], in_=xt,
                          func=AF.Copy, accum_out=col(pa))
                S.wait("scalar", s1)
                s2 = S.op("scalar", "activation", out=scr[:, :], in_=xt,
                          func=AF.Square, accum_out=col(pb))
                S.wait("vector", s1, s2)
                mm = S.op("vector", "tensor_scalar", out=col(CM_M + t),
                          in0=col(pa), scalar1=1.0 / D, scalar2=None, op0=Op.mult)
                S.wait("vector", mm)
                t2 = S.op("vector", "tensor_tensor", out=col(pc),
                          in0=col(CM_M + t), in1=col(CM_M + t), op=Op.mult)
                S.wait("vector", t2)
                vv = S.op("vector", "scalar_tensor_tensor", out=col(pc),
                          in0=col(pb), scalar=1.0 / D, in1=col(pc),
                          op0=Op.mult, op1=Op.subtract)
                S.wait("scalar", vv)
                sq = S.op("scalar", "activation", out=col(pc), in_=col(pc),
                          func=AF.Sqrt, bias=EPS)
                vv_hist[t % 2] = sq
                S.wait("vector", sq)
                rr = S.op("vector", "reciprocal", out=col(CM_R + t),
                          in_=col(pc))
                if want_minmax:
                    S.op("vector", "tensor_reduce", out=col(CM_XMX + t), in_=xt,
                         axis=AX.X, op=Op.max)
                    rr = S.op("vector", "tensor_reduce", out=col(CM_XMN + t),
                              in_=xt, axis=AX.X, op=Op.min)
                recs.append(rr)
            return recs

        def emit_armax(cc_idx, src_ap, dep_recs):
            z = S.op("vector", "memset", ap=A_row[0:1, 16:24], constant=0.0)
            S.wait("gpsimd", z, *dep_recs)
            pr = S.op("gpsimd", "tensor_reduce", out=A_row[0:1, 16:17],
                      in_=src_ap, axis=AX.XYZWC, op=Op.max)
            S.wait("gpsimd", pr, war)
            d1 = S.dma("gpsimd", "mx", out=mx_in[cc_idx].ap(), in_=A_row[0:1, 16:24])
            S.wait("gpsimd", d1)
            ar = S.cc(lambda g, i=cc_idx: g.collective_compute(
                "AllReduce", Op.max, replica_groups=[list(range(8))],
                ins=[mx_in[i].ap().opt()], outs=[mx_out[i].ap().opt()]))
            S.wait("gpsimd", ar)
            return S.dma("gpsimd", "mx", out=A_row[0:1, 24:32], in_=mx_out[cc_idx].ap())

        def emit_gmax_cols():
            last = None
            for t in range(NTT):
                S.wait("vector", last)
                S.op("vector", "tensor_tensor", out=col(CM_T0),
                     in0=col(CM_XMX + t), in1=col(CM_M + t), op=Op.subtract)
                u2 = S.op("vector", "tensor_tensor", out=col(CM_T0 + 1),
                          in0=col(CM_M + t), in1=col(CM_XMN + t), op=Op.subtract)
                S.wait("vector", u2)
                u3 = S.op("vector", "tensor_tensor", out=col(CM_T0),
                          in0=col(CM_T0), in1=col(CM_T0 + 1), op=Op.max)
                S.wait("vector", u3)
                last = S.op("vector", "tensor_tensor", out=col(CM_AM + t),
                            in0=col(CM_T0), in1=col(CM_R + t), op=Op.mult)
            return last

        def emit_scale_cols(gmax_rec, clip_off, sx_col, isx_col):
            S.wait("scalar", gmax_rec, ld_clip, ld_clipd, ld_gm)
            sg = S.op("scalar", "activation", out=A_row[0:1, 32:33],
                      in_=A_row[0:1, clip_off:clip_off + 1], func=AF.Sigmoid)
            S.wait("vector", sg, gmax_rec)
            w1 = S.op("vector", "tensor_tensor", out=A_row[0:1, 33:34],
                      in0=A_row[0:1, 32:33], in1=A_row[0:1, 24:25], op=Op.mult)
            S.wait("vector", w1)
            sm = S.op("vector", "tensor_scalar", out=A_row[0:1, 34:35],
                      in0=A_row[0:1, 33:34], scalar1=1.0 / QMAX, scalar2=None,
                      op0=Op.mult)
            S.wait("vector", sm)
            iv = S.op("vector", "reciprocal", out=A_row[0:1, 35:36],
                      in_=A_row[0:1, 34:35])
            on = S.op("vector", "memset", ap=A_row[0:1, 40:168], constant=1.0)
            S.wait("tensor", sm, iv, on)
            m1 = S.op("tensor", "matmul", out=PS[6][0:128, 0:1],
                      lhsT=A_row[0:1, 40:168], rhs=A_row[0:1, 34:35],
                      start=True, stop=True)
            m2 = S.op("tensor", "matmul", out=PS[6][0:128, 1:2],
                      lhsT=A_row[0:1, 40:168], rhs=A_row[0:1, 35:36],
                      start=True, stop=True)
            S.wait("vector", m1, m2)
            c1 = S.op("vector", "tensor_copy", out=col(sx_col), in_=PS[6][:, 0:1])
            c2 = S.op("vector", "tensor_copy", out=col(isx_col), in_=PS[6][:, 1:2])
            return c1, c2

        def emit_ffn(first_call, cc0, out_final):
            with ExitStack() as ps:
                wgf = psb(ps, f"wgf{cc0}", [128, NKD * IFF], BF16)
                wuf = psb(ps, f"wuf{cc0}", [128, NKD * IFF], BF16)
                xqf = psb(ps, f"xqf{cc0}", [128, NKD * 1024], BF16)
                tmp = psb(ps, f"ftmp{cc0}", [128, D], F32)
                xqb = psb(ps, f"fxqb{cc0}", [128, D], BF16)
                hbuf = [psb(ps, f"hb{cc0}_{i}", [128, 1024], BF16) for i in range(2)]
                silu = psb(ps, f"silu{cc0}", [128, 512], F32)

                ln_recs = emit_ln_stats(tmp, True)
                am_last = emit_gmax_cols()
                gm = emit_armax(cc0, cols(CM_AM, NTT), [am_last])
                c1, c2 = emit_scale_cols(gm, 0, CM_SX, CM_ISX)

                if first_call:
                    sp_gu, sp_d = emit_weight_quant(ps)
                else:
                    sp_gu, sp_d = [], []

                if first_call:
                    S.wait("sync", *sp_gu)
                trw = []
                for kk in range(NKD):
                    trw.append(S.dma_tr("sync", "trw", out=wgf[:, kk * IFF:(kk + 1) * IFF],
                                        in_=wdg_bf[:, kk * 128:(kk + 1) * 128]))
                    trw.append(S.dma_tr("sync", "trw", out=wuf[:, kk * IFF:(kk + 1) * IFF],
                                        in_=wdu_bf[:, kk * 128:(kk + 1) * 128]))

                S.wait("scalar", c1, c2)
                sp_xq = []
                prev_a1 = None
                for t in range(NTT):
                    xt = A_x[:, t * D:(t + 1) * D]
                    S.wait("vector", prev_a1)
                    q1 = S.op("vector", "tensor_scalar", out=tmp[:, :], in0=xt,
                              scalar1=col(CM_M + t), scalar2=col(CM_R + t),
                              op0=Op.subtract, op1=Op.mult)
                    S.wait("scalar", q1,
                           *( [sp_d[-1]] if (t == 0 and sp_d) else []))
                    if t >= 1:
                        S.wait("scalar", sp_xq[t - 1])
                    a1 = S.op("scalar", "activation", out=A_i8[:, 0:D],
                              in_=tmp[:, :], func=AF.Copy, scale=col(CM_ISX))
                    prev_a1 = a1
                    S.wait("scalar", a1)
                    a2 = S.op("scalar", "activation", out=xqb[:, :],
                              in_=A_i8[:, 0:D], func=AF.Copy, scale=col(CM_SX))
                    S.wait("sync", a2)
                    sp_xq.append(S.dma("sync", "xqs", out=xq_d[t * 128:(t + 1) * 128, :],
                                       in_=xqb[:, :]))

                hm0 = S.op("vector", "memset", ap=col(CM_HMX), constant=0.0)
                ev = {}
                mm_last = None
                sl_prev = None
                hx_last = None
                g = 0
                for half in range(2):
                    S.wait("sync", *sp_xq)
                    if half == 1:
                        S.wait("sync", mm_last)
                    trx = []
                    for kk in range(NKD):
                        trx.append(S.dma_tr(
                            "sync", "trx", out=xqf[:, kk * 1024:(kk + 1) * 1024],
                            in_=xq_d[half * 1024:(half + 1) * 1024,
                                     kk * 128:(kk + 1) * 128]))
                    for ic in range(NIC):
                        hb = hbuf[ic % 2]
                        for tc in range(2):
                            pg, pu = PS[(g % 2) * 2], PS[(g % 2) * 2 + 1]
                            if g >= 2:
                                S.wait("tensor", ev[g - 2])
                            if g == 0:
                                S.wait("tensor", *(trw + trx), c1, c2)
                            elif half == 1 and ic == 0 and tc == 0:
                                S.wait("tensor", *trx)
                            for kk in range(NKD):
                                mg = S.op("tensor", "matmul", out=pg[:, :],
                                          lhsT=wgf[:, kk * IFF + ic * 128:
                                                   kk * IFF + (ic + 1) * 128],
                                          rhs=xqf[:, kk * 1024 + tc * 512:
                                                  kk * 1024 + (tc + 1) * 512],
                                          start=(kk == 0), stop=(kk == NKD - 1))
                            for kk in range(NKD):
                                mu = S.op("tensor", "matmul", out=pu[:, :],
                                          lhsT=wuf[:, kk * IFF + ic * 128:
                                                   kk * IFF + (ic + 1) * 128],
                                          rhs=xqf[:, kk * 1024 + tc * 512:
                                                  kk * 1024 + (tc + 1) * 512],
                                          start=(kk == 0), stop=(kk == NKD - 1))
                            mm_last = mu
                            S.wait("scalar", mg, sl_prev)
                            sl = S.op("scalar", "activation", out=silu[:, :],
                                      in_=pg[:, :], func=AF.Silu)
                            S.wait("vector", sl, mu, hm0)
                            if tc == 0:
                                S.wait("vector", ev.get(("sp", half * NIC + ic - 2)))
                            hh = S.op("vector", "scalar_tensor_tensor",
                                      out=hb[:, tc * 512:(tc + 1) * 512],
                                      in0=silu[:, :], scalar=1.0, in1=pu[:, :],
                                      op0=Op.mult, op1=Op.mult)
                            sl_prev = hh
                            ev[g] = hh
                            g += 1
                        S.wait("vector", hh)
                        rd = S.op("vector", "tensor_reduce",
                                  out=col(CM_T0 + 11), in_=hb[:, :],
                                  axis=AX.X, op=Op.max,
                                  apply_absolute_value=True)
                        S.wait("vector", rd)
                        hx_last = S.op("vector", "tensor_tensor",
                                       out=col(CM_HMX), in0=col(CM_HMX),
                                       in1=col(CM_T0 + 11), op=Op.max)
                        ev[("hx", half * NIC + ic)] = hx_last
                        S.wait("sync", ev[g - 1])
                        ev[("sp", half * NIC + ic)] = S.dma(
                            "sync", f"hs{ic % 2}",
                            out=h_d[ic * 128:(ic + 1) * 128,
                                    half * 1024:(half + 1) * 1024],
                            in_=hb[:, :])

                hm = emit_armax(cc0 + 1, col(CM_HMX), [hx_last])
                c3, c4 = emit_scale_cols(hm, 1, CM_SH, CM_ISH)

                S.wait("sync", mm_last)
                if first_call:
                    S.wait("sync", *sp_d)
                trd = [S.dma_tr("sync", "trw", out=wgf[:, ic * D:(ic + 1) * D],
                                in_=wdd_bf[:, ic * 128:(ic + 1) * 128])
                       for ic in range(NIC)]
                NSL = 4
                hsl_view = wuf[:, 0:NSL * 2 * IFF].rearrange(
                    "p (s c f) -> p s c f", s=NSL, c=NIC)
                hd_view = h_d.rearrange("(c p) t -> p c t", p=128)
                srec = {}
                a2_prev = None
                ldh = None
                for t in range(NTT):
                    sl4, hh2, sl2 = (t // 2) % NSL, t % 2, t % 2
                    if t == 0:
                        S.wait("sync", ev[("sp", 2 * NIC - 1)],
                               ev[("sp", 2 * NIC - 2)])
                    if t % 2 == 0:
                        if t >= 2 * NSL:
                            S.wait("sync", srec[("a1", t - 2 * NSL + 1)])
                        ldh = S.dma("sync", f"dh{sl4}",
                                    out=hsl_view[:, sl4, :, :],
                                    in_=hd_view[:, :, t * 128:(t + 2) * 128])
                    S.wait("scalar", ldh, a2_prev,
                           *([c3, c4, mm_last] if t == 0 else []))
                    if t >= 2:
                        S.wait("scalar", srec[("mm", t - 2)])
                    a1 = S.op("scalar", "activation",
                              out=A_i8[:, 0:IFF].rearrange(
                                  "p (c f) -> p c f", c=NIC),
                              in_=hsl_view[:, sl4, :, hh2 * 128:(hh2 + 1) * 128],
                              func=AF.Copy, scale=col(CM_ISH))
                    srec[("a1", t)] = a1
                    S.wait("scalar", a1)
                    a2 = S.op("scalar", "activation",
                              out=xqf[:, sl2 * IFF:(sl2 + 1) * IFF],
                              in_=A_i8[:, 0:IFF], func=AF.Copy, scale=col(CM_SH))
                    a2_prev = a2
                    pd = (PS[4], PS[5]) if sl2 == 0 else (PS[6], PS[3])
                    S.wait("tensor", a2)
                    if t == 0:
                        S.wait("tensor", *trd)
                    if t == 1:
                        S.wait("tensor", sl_prev)
                    if t >= 2:
                        S.wait("tensor", srec[("ev", t - 2)])
                    md = None
                    for ic in range(NIC):
                        S.op("tensor", "matmul", out=pd[0][:, :],
                             lhsT=xqf[:, sl2 * IFF + ic * 128:
                                      sl2 * IFF + (ic + 1) * 128],
                             rhs=wgf[:, ic * D: ic * D + 512],
                             start=(ic == 0), stop=(ic == NIC - 1))
                        md = S.op("tensor", "matmul", out=pd[1][:, :],
                                  lhsT=xqf[:, sl2 * IFF + ic * 128:
                                           sl2 * IFF + (ic + 1) * 128],
                                  rhs=wgf[:, ic * D + 512: (ic + 1) * D],
                                  start=(ic == 0), stop=(ic == NIC - 1))
                    srec[("mm", t)] = md
                    S.wait("vector", md)
                    S.op("vector", "tensor_tensor",
                         out=A_x[:, t * D: t * D + 512],
                         in0=A_x[:, t * D: t * D + 512], in1=pd[0][:, :],
                         op=Op.add)
                    e2 = S.op("vector", "tensor_tensor",
                              out=A_x[:, t * D + 512: (t + 1) * D],
                              in0=A_x[:, t * D + 512: (t + 1) * D],
                              in1=pd[1][:, :], op=Op.add)
                    srec[("ev", t)] = e2
                    if out_final:
                        S.wait("sync", e2)
                        srec[("out", t)] = S.dma(
                            "sync", "out", out=out[t * 128:(t + 1) * 128, :],
                            in_=A_x[:, t * D:(t + 1) * D])
                return srec


        def emit_attn():
            nonlocal A_x
            with ExitStack() as aps:
                A_Sb = psb(aps, "A_Sb", [128, NKD * DVH], BF16)
                A_Sin = psb(aps, "A_Sin", [128, NKD * DVH], F32)
                A_Sinb = A_Sb
                A_msk = psb(aps, "A_msk", [128, 2 * CHK], BF16)
                A_zt = psb(aps, "A_zt", [128, CHK], F32)
                A_id = psb(aps, "A_id", [128, 128], BF16)
                tid_f = psb(aps, "tid_f", [128, 128], F32)

                z0 = S.op("vector", "memset", ap=A_zt[:, :], constant=0.0)
                z2_ = S.op("gpsimd", "memset", ap=A_Sb[:, :], constant=0.0)
                dz = S.op("vector", "memset", ap=cols(CM_DAC, 8), constant=1.0)
                mk0 = S.op("gpsimd", "iota", out=A_msk[:, 0:CHK], pattern=[[1, CHK]],
                           base=0, channel_multiplier=-1,
                           allow_small_or_imprecise_dtypes=True)
                mk1 = S.op("gpsimd", "iota", out=A_msk[:, CHK:2 * CHK],
                           pattern=[[1, CHK]], base=-128, channel_multiplier=-1,
                           allow_small_or_imprecise_dtypes=True)
                S.wait("gpsimd", mk0, mk1)
                mk2 = S.op("gpsimd", "tensor_scalar", out=A_msk[:, :],
                           in0=A_msk[:, :], scalar1=0.0, scalar2=1.0 / 16.0,
                           op0=Op.is_ge, op1=Op.mult)
                id0 = S.op("gpsimd", "iota", out=tid_f[:, :], pattern=[[1, 128]],
                           base=0, channel_multiplier=-1,
                           allow_small_or_imprecise_dtypes=True)
                S.wait("gpsimd", id0)
                id1 = S.op("gpsimd", "tensor_scalar", out=A_id[:, :],
                           in0=tid_f[:, :], scalar1=0.0, scalar2=None,
                           op0=Op.is_equal)

                with ExitStack() as p1:
                    tmp2 = psb(p1, "atmp", [128, D], F32)
                    xn1b = psb(p1, "axnb", [128, D], BF16)
                    xn1f = psb(p1, "xn1f", [128, NKD * TOK], BF16)
                    Wa = psb(p1, "Wa", [128, NKD * VD], BF16)
                    stg2 = [psb(p1, f"stg{i}", [128, TOK], BF16) for i in range(2)]
                    zf = psb(p1, "zf", [128, 512], F32)
                    za = psb(p1, "za", [128, 512], F32)
                    zl = psb(p1, "zl", [128, 512], F32)
                    z1b = psb(p1, "z1b", [128, TOK], BF16)
                    gk1a = psb(p1, "gk1a", [128, NKD * 16], BF16)
                    gk2a = psb(p1, "gk2a", [16, KD], BF16)
                    b2c = psb(p1, "b2c", [128, NKD], F32)

                    emit_ln_stats(tmp2, False)
                    sp_xn, sp_x1 = [], []
                    prev_sp = None
                    for t in range(NTT):
                        xt = A_x[:, t * D:(t + 1) * D]
                        S.wait("vector", prev_sp)
                        qn = S.op("vector", "tensor_scalar", out=xn1b[:, :],
                                  in0=xt, scalar1=col(CM_M + t),
                                  scalar2=col(CM_R + t), op0=Op.subtract,
                                  op1=Op.mult)
                        S.wait("sync", qn)
                        s1 = S.dma("sync", "xns", out=xn1_d[t * 128:(t + 1) * 128, :],
                                   in_=xn1b[:, :])
                        prev_sp = s1
                        sp_xn.append(s1)
                        sp_x1.append(S.dma("sync", "x1s", out=x1_d[t * 128:(t + 1) * 128, :],
                                           in_=xt))
                    S.wait("sync", *sp_xn)
                    trx = [S.dma_tr("sync", "trx", out=xn1f[:, kk * TOK:(kk + 1) * TOK],
                                    in_=xn1_d[:, kk * 128:(kk + 1) * 128])
                           for kk in range(NKD)]
                    cmX1.__exit__(None, None, None)
                    lgk1 = [S.dma("gpsimd", "gk", out=gk1a[:, kk * 16:(kk + 1) * 16],
                                  in_=gk1[kk * 128:(kk + 1) * 128, :])
                            for kk in range(NKD)]
                    lgk2 = S.dma("gpsimd", "gk", out=gk2a[:, :], in_=gk2[:, :])
                    lb2 = [S.dma("sync", "gk2", out=b2c[:, kk:kk + 1],
                                 in_=gkb[kk * 128:(kk + 1) * 128][:, None])
                           for kk in range(NKD)]

                    def prepass_fm(wsrc, wcols, dst, act=None, prev=None):
                        if prev:
                            S.wait("gpsimd", prev["mm"])
                        lw = [S.dma("gpsimd", "pw",
                                    out=Wa[:, kk * wcols:(kk + 1) * wcols],
                                    in_=wsrc[kk * 128:(kk + 1) * 128, :])
                              for kk in range(NKD)]
                        nf = wcols // 128
                        sp = []
                        pe_prev = {}
                        mm = None
                        for fc in range(nf):
                            sg = stg2[fc % 2]
                            for tc4 in range(4):
                                pp = PS[(fc * 4 + tc4) % 2]
                                if (fc, tc4) == (0, 0):
                                    S.wait("tensor", *(lw + trx),
                                           *( [prev["ev"]] if prev else []))
                                k = (fc * 4 + tc4) % 2
                                if fc * 4 + tc4 >= 2:
                                    S.wait("tensor", pe_prev[k])
                                for kk in range(NKD):
                                    mm = S.op("tensor", "matmul", out=pp[:, :],
                                              lhsT=Wa[:, kk * wcols + fc * 128:
                                                      kk * wcols + (fc + 1) * 128],
                                              rhs=xn1f[:, kk * TOK + tc4 * 512:
                                                       kk * TOK + (tc4 + 1) * 512],
                                              start=(kk == 0), stop=(kk == NKD - 1))
                                S.wait("scalar", mm)
                                if fc <= 1 and tc4 == 0 and prev:
                                    S.wait("scalar", prev["sp2"] if fc == 0 else prev["sp"])
                                if fc > 1 and tc4 == 0:
                                    S.wait("scalar", sp[-2])
                                ev = S.op("scalar", "activation",
                                          out=sg[:, tc4 * 512:(tc4 + 1) * 512],
                                          in_=pp[:, :],
                                          func=(act or AF.Copy))
                                pe_prev[k] = ev
                            S.wait("sync", pe_prev[(fc * 4 + 3) % 2])
                            sp.append(S.dma("sync", f"pps{fc % 2}",
                                            out=dst[fc * 128:(fc + 1) * 128, :],
                                            in_=sg[:, :]))
                        return dict(sp=sp[-1], sp2=sp[-2], sps=sp, mm=mm, ev=ev)

                    def prepass_tm(wsrc, dst, act=None, prev=None):
                        if prev:
                            S.wait("gpsimd", prev["mm"])
                        lw = [S.dma("gpsimd", "pw", out=Wa[:, kk * VD:(kk + 1) * VD],
                                    in_=wsrc[kk * 128:(kk + 1) * 128, :])
                              for kk in range(NKD)]
                        sp = []
                        pe_prev = {}
                        mm = None
                        for t in range(NTT):
                            sg = stg2[t % 2]
                            for vc in range(4):
                                pp = PS[(t * 4 + vc) % 2]
                                if (t, vc) == (0, 0):
                                    S.wait("tensor", *(lw + trx),
                                           *( [prev["ev"]] if prev else []))
                                k = (t * 4 + vc) % 2
                                if t * 4 + vc >= 2:
                                    S.wait("tensor", pe_prev[k])
                                for kk in range(NKD):
                                    mm = S.op("tensor", "matmul", out=pp[:, :],
                                              lhsT=xn1f[:, kk * TOK + t * 128:
                                                        kk * TOK + (t + 1) * 128],
                                              rhs=Wa[:, kk * VD + vc * 512:
                                                     kk * VD + (vc + 1) * 512],
                                              start=(kk == 0), stop=(kk == NKD - 1))
                                S.wait("scalar", mm)
                                if t <= 1 and vc == 0 and prev:
                                    S.wait("scalar", prev["sp2"] if t == 0 else prev["sp"])
                                if t > 1 and vc == 0:
                                    S.wait("scalar", sp[-2])
                                ev = S.op("scalar", "activation",
                                          out=sg[:, vc * 512:(vc + 1) * 512],
                                          in_=pp[:, :], func=(act or AF.Copy))
                                pe_prev[k] = ev
                            S.wait("sync", pe_prev[(t * 4 + 3) % 2])
                            sp.append(S.dma("sync", f"pps{t % 2}",
                                            out=dst[t * 128:(t + 1) * 128, :],
                                            in_=sg[:, :]))
                        return dict(sp=sp[-1], sp2=sp[-2], sps=sp, mm=mm, ev=ev)

                    pq = prepass_fm(Wq, 1024, q_d)
                    pk = prepass_fm(Wk, 1024, k_d, prev=pq)
                    pv = prepass_tm(Wv, v_d, prev=pk)
                    psg = prepass_tm(Wg, sg_d, act=AF.Silu, prev=pv)

                    z1evs = []
                    for tc4 in range(4):
                        pp = PS[2]
                        if tc4 > 0:
                            S.wait("tensor", z1evs[-1])
                        if tc4 == 0:
                            S.wait("tensor", *lgk1, lgk2)
                        mm = None
                        for kk in range(NKD):
                            mm = S.op("tensor", "matmul", out=pp[0:16, :],
                                      lhsT=gk1a[:, kk * 16:(kk + 1) * 16],
                                      rhs=xn1f[:, kk * TOK + tc4 * 512:
                                               kk * TOK + (tc4 + 1) * 512],
                                      start=(kk == 0), stop=(kk == NKD - 1))
                        S.wait("scalar", mm)
                        ev = S.op("scalar", "activation",
                                  out=z1b[0:16, tc4 * 512:(tc4 + 1) * 512],
                                  in_=pp[0:16, :], func=AF.Copy)
                        z1evs.append(ev)
                    gsp = []
                    gprev = None
                    gprev_pair = {}
                    for fc in range(NKD):
                        for tc4 in range(4):
                            pp = PS[3 + ((fc * 4 + tc4) % 2)]
                            if fc * 4 + tc4 >= 2:
                                S.wait("tensor", gprev_pair[(fc * 4 + tc4) % 2])
                            if (fc, tc4) == (0, 0):
                                S.wait("tensor", z1evs[-1], lgk2)
                            if tc4 == 0 and fc <= 1:
                                S.wait("vector", psg["sp2"] if fc == 0 else psg["sp"])
                            mz = S.op("tensor", "matmul", out=pp[:, :],
                                      lhsT=gk2a[:, fc * 128:(fc + 1) * 128],
                                      rhs=z1b[0:16, tc4 * 512:(tc4 + 1) * 512],
                                      start=True, stop=True)
                            S.wait("vector", mz, *( [lb2[-1]] if tc4 == 0 else []))
                            if gprev is not None:
                                S.wait("vector", gprev)
                            zb = S.op("vector", "tensor_scalar", out=zf[:, :],
                                      in0=pp[:, :], scalar1=b2c[:, fc:fc + 1],
                                      scalar2=None, op0=Op.add)
                            S.wait("scalar", zb)
                            av = S.op("scalar", "activation", out=za[:, :],
                                      in_=zf[:, :], func=AF.Abs)
                            S.wait("scalar", av)
                            ee = S.op("scalar", "activation", out=za[:, :],
                                      in_=za[:, :], func=AF.Exp, scale=-1.0)
                            S.wait("scalar", ee)
                            le = S.op("scalar", "activation", out=zl[:, :],
                                      in_=za[:, :], func=AF.Ln, bias=1.0)
                            S.wait("vector", le)
                            mn = S.op("vector", "tensor_scalar", out=zf[:, :],
                                      in0=zf[:, :], scalar1=0.0, scalar2=None,
                                      op0=Op.min)
                            S.wait("vector", mn)
                            gg = S.op("vector", "tensor_tensor",
                                      out=stg2[fc % 2][:, tc4 * 512:(tc4 + 1) * 512],
                                      in0=zf[:, :], in1=zl[:, :], op=Op.subtract)
                            gprev = gg
                            gprev_pair[(fc * 4 + tc4) % 2] = gg
                        S.wait("sync", gprev)
                        gsp.append(S.dma("sync", f"pps{fc % 2}",
                                         out=g_d[fc * 128:(fc + 1) * 128, :],
                                         in_=stg2[fc % 2][:, :]))
                        if fc >= 1:
                            S.wait("vector", gsp[-2])
                    S.wait("sync", *(psg["sps"] + gsp))
                S.barrier()

                with ExitStack() as p2:
                    A_S = psb(p2, "A_S", [128, NKD * DVH], F32)
                    zS = S.op("vector", "memset", ap=A_S[:, :], constant=0.0)
                    db = []
                    for i in range(2):
                        db.append(dict(
                            b=psb(p2, f"b{i}", [128, 8 * CHK], F32),
                            g=psb(p2, f"g{i}", [128, 8 * CHK], BF16),
                            q=psb(p2, f"q{i}", [128, 8 * CHK], BF16),
                            k=psb(p2, f"k{i}", [128, 8 * CHK], BF16),
                            eb=psb(p2, f"eb{i}", [128, 8 * CHK], BF16),
                            emb=psb(p2, f"em{i}", [128, 8 * CHK], BF16),
                            qr=psb(p2, f"qr{i}", [128, 8 * CHK], BF16),
                            kr=psb(p2, f"kr{i}", [128, 8 * CHK], BF16),
                            qf=psb(p2, f"qf{i}", [128, 8 * CHK], BF16),
                            v=[psb(p2, f"v{i}_{j}", [128, VD], BF16)
                               for j in range(2)],
                        ))
                    A_kh = psb(p2, "A_kh", [128, 8 * CHK], BF16)
                    A_qg = psb(p2, "A_qg", [128, 8 * CHK], BF16)
                    A_ktr = psb(p2, "A_ktr", [128, 2 * KD], BF16)
                    A_pm = psb(p2, "A_pm", [128, 2 * CHK], BF16)
                    A_ol = [psb(p2, f"A_ol{i}", [128, VD], BF16) for i in range(2)]

                    gv = g_d.rearrange("(c p) t -> p c t", p=128)
                    qv = q_d.rearrange("(c p) t -> p c t", p=128)
                    kv = k_d.rearrange("(c p) t -> p c t", p=128)
                    qgv = qg_d.rearrange("(c p) t -> p c t", p=128)
                    khv = kh_d.rearrange("(c p) t -> p c t", p=128)
                    qgsv = A_qg.rearrange("p (c t) -> p c t", t=CHK)
                    khsv = A_kh.rearrange("p (c t) -> p c t", t=CHK)

                    prevc = {}
                    for c in range(NCH):
                        if PARANOID:
                            S.barrier()
                        cp = c % 2
                        B = db[cp]
                        base = c * CHK
                        if c > 1:
                            S.wait("sync", prevc[("scan", cp)],
                                   prevc[("qglast", cp)], prevc[("vlast", cp)])
                        ldg = S.dma("sync", f"cg{cp}",
                                    out=B["g"].rearrange("p (c t) -> p c t", t=CHK),
                                    in_=gv[:, :, base:base + CHK])
                        ldq = S.dma("sync", f"cq{cp}",
                                    out=B["q"].rearrange("p (c t) -> p c t", t=CHK),
                                    in_=qv[:, :, base:base + CHK])
                        ldk = S.dma("sync", f"ck{cp}",
                                    out=B["k"].rearrange("p (c t) -> p c t", t=CHK),
                                    in_=kv[:, :, base:base + CHK])
                        ldv = [S.dma("sync", f"cv{cp}{i}", out=B["v"][i][:, :],
                                     in_=v_d[base + i * 128:base + (i + 1) * 128, :])
                               for i in range(2)]
                        ebx = prevc.get(("ebx", cp))
                        S.wait("vector", ldg, *(list(ebx) if ebx else []))
                        sc = None
                        for fc in range(NKD):
                            sc = S.op("vector", "tensor_tensor_scan",
                                      out=B["b"][:, fc * CHK:(fc + 1) * CHK],
                                      data0=B["g"].rearrange(
                                          "p (c t) -> p c t", t=CHK)[:, fc, :],
                                      data1=A_zt[:, :], initial=0.0,
                                      op0=Op.add, op1=Op.add)
                        prevc[("scan", cp)] = sc
                        S.wait("scalar", sc, *(prevc.get(("qk", cp)) or []))
                        ebr = S.op("scalar", "activation", out=B["eb"][:, :],
                                   in_=B["b"][:, :], func=AF.Exp, scale=1.0 / 16)
                        embr = S.op("scalar", "activation", out=B["emb"][:, :],
                                    in_=B["b"][:, :], func=AF.Exp, scale=-1.0 / 16)
                        if c > 0:
                            S.wait("scalar", prevc[("stup", 7)])
                        ebcr = S.op("scalar", "activation", out=cols(CM_EBC, 8),
                                    in_=B["b"].rearrange(
                                        "p (f t) -> p f t", t=CHK)[:, :, CHK - 1],
                                    func=AF.Exp, scale=1.0 / 16)
                        prevc[("ebx", cp)] = (ebr, embr, ebcr)
                        S.wait("vector", ldq, ldk, ebr, embr,
                               *( [prevc[("m2last", cp)], prevc[("molast", cp)]]
                                  if c > 1 else []))
                        qrr = S.op("vector", "tensor_tensor", out=B["qr"][:, :],
                                   in0=B["q"][:, :], in1=B["eb"][:, :], op=Op.mult)
                        krr = S.op("vector", "tensor_tensor", out=B["kr"][:, :],
                                   in0=B["k"][:, :], in1=B["emb"][:, :], op=Op.mult)
                        S.wait("vector", qrr)
                        qfr = S.op("vector", "tensor_scalar", out=B["qf"][:, :],
                                   in0=B["qr"][:, :], scalar1=1.0 / 16,
                                   scalar2=None, op0=Op.mult)
                        prevc[("qk", cp)] = (qrr, krr, qfr)
                        S.wait("vector", krr, ebcr,
                               *( [prevc["spkh"], prevc["spqg"]] if c > 0 else []))
                        khl = None
                        for fc in range(NKD):
                            khl = S.op("vector", "tensor_scalar",
                                       out=A_kh[:, fc * CHK:(fc + 1) * CHK],
                                       in0=B["kr"][:, fc * CHK:(fc + 1) * CHK],
                                       scalar1=col(CM_EBC + fc), scalar2=None,
                                       op0=Op.mult)
                        S.wait("vector", qfr)
                        qgl = None
                        for fc in range(NKD):
                            qgl = S.op("vector", "tensor_scalar",
                                       out=A_qg[:, fc * CHK:(fc + 1) * CHK],
                                       in0=B["qf"][:, fc * CHK:(fc + 1) * CHK],
                                       scalar1=col(CM_DAC + fc), scalar2=None,
                                       op0=Op.mult)
                        prevc[("qglast", cp)] = qgl
                        S.wait("vector", qgl)
                        dacu = S.op("vector", "tensor_tensor",
                                    out=cols(CM_DAC, 8), in0=cols(CM_DAC, 8),
                                    in1=cols(CM_EBC, 8), op=Op.mult)
                        prevc["dacu"] = dacu
                        prevc["dacu_last"] = dacu
                        S.wait("sync", qgl, khl)
                        spqg = S.dma("sync", "sqg", out=qgv[:, :, base:base + CHK],
                                     in_=qgsv[:, :, :])
                        if c > 0:
                            S.wait("sync", prevc["ktr"][1], prevc[("vlast", 1 - cp)])
                        spkh = S.dma("sync", "skh", out=khv[:, :, :],
                                     in_=khsv[:, :, :])
                        prevc["spqg"] = spqg
                        prevc["spkh"] = spkh
                        S.wait("sync", spkh)
                        ktr = [S.dma_tr("sync", "ktr",
                                        out=A_ktr[:, i * KD:(i + 1) * KD],
                                        in_=kh_d[:, i * 128:(i + 1) * 128])
                               for i in range(2)]
                        prevc["ktr"] = ktr

                        o_evs = []
                        for h in range(NH):
                            pmevs = []
                            for sb2 in range(2):
                                if (c, h, sb2) != (0, 0, 0):
                                    S.wait("tensor", prevc.get("pmev"))
                                else:
                                    S.wait("tensor", mk2, zS, z2_)
                                    S.wait("vector", mk2)
                                if h == 0 and sb2 == 0:
                                    S.wait("tensor", qgl, ldv[0], ldv[1])
                                if sb2 == 0 and prevc.get("molast") is not None:
                                    S.wait("vector", prevc["molast"])
                                m1 = S.op("tensor", "matmul",
                                          out=PS[2][:, 0:CHK],
                                          lhsT=B["kr"][:, (2 * h) * CHK + sb2 * 128:
                                                       (2 * h) * CHK + sb2 * 128 + 128],
                                          rhs=B["qr"][:, (2 * h) * CHK:(2 * h + 1) * CHK],
                                          start=True, stop=False)
                                m2 = S.op("tensor", "matmul",
                                          out=PS[2][:, 0:CHK],
                                          lhsT=B["kr"][:, (2 * h + 1) * CHK + sb2 * 128:
                                                       (2 * h + 1) * CHK + sb2 * 128 + 128],
                                          rhs=B["qr"][:, (2 * h + 1) * CHK:(2 * h + 2) * CHK],
                                          start=False, stop=True)
                                prevc[("m2last", cp)] = m2
                                prevc["m2last"] = m2
                                S.wait("vector", m2)
                                pmev = S.op("vector", "tensor_tensor",
                                            out=A_pm[:, sb2 * CHK:(sb2 + 1) * CHK],
                                            in0=PS[2][:, 0:CHK],
                                            in1=A_msk[:, sb2 * CHK:(sb2 + 1) * CHK],
                                            op=Op.mult)
                                prevc["pmev"] = pmev
                                pmevs.append(pmev)
                            for th in range(2):
                                po = PS[3 + th]
                                S.wait("tensor", pmevs[0], pmevs[1],
                                       prevc.get(("olastuse", th)),
                                       *( [prevc.get(("sbcp", 2 * h)),
                                           prevc.get(("sbcp", 2 * h + 1))]
                                          if c > 0 else []))
                                S.op("tensor", "matmul", out=po[:, :],
                                     lhsT=A_pm[:, 0 * CHK + th * 128:
                                               0 * CHK + th * 128 + 128],
                                     rhs=B["v"][0][:, h * DVH:(h + 1) * DVH],
                                     start=True, stop=False)
                                S.op("tensor", "matmul", out=po[:, :],
                                     lhsT=A_pm[:, 1 * CHK + th * 128:
                                               1 * CHK + th * 128 + 128],
                                     rhs=B["v"][1][:, h * DVH:(h + 1) * DVH],
                                     start=False, stop=False)
                                S.op("tensor", "matmul", out=po[:, :],
                                     lhsT=B["qf"][:, (2 * h) * CHK + th * 128:
                                                  (2 * h) * CHK + th * 128 + 128],
                                     rhs=A_Sb[:, (2 * h) * DVH:(2 * h + 1) * DVH],
                                     start=False, stop=False)
                                mo = S.op("tensor", "matmul", out=po[:, :],
                                          lhsT=B["qf"][:, (2 * h + 1) * CHK + th * 128:
                                                       (2 * h + 1) * CHK + th * 128 + 128],
                                          rhs=A_Sb[:, (2 * h + 1) * DVH:(2 * h + 2) * DVH],
                                          start=False, stop=True)
                                S.wait("scalar", mo)
                                if c > 0 or h > 0:
                                    S.wait("scalar", prevc.get(("osp", th)))
                                oev = S.op("scalar", "activation",
                                           out=A_ol[th][:, h * DVH:(h + 1) * DVH],
                                           in_=po[:, :], func=AF.Copy)
                                prevc[("oev", th, h)] = oev
                                prevc[("olastuse", th)] = oev
                                prevc[("olast", th)] = oev
                                o_evs.append(oev)
                                prevc[("mo", h, th)] = mo
                            for fsub in range(2):
                                fc = 2 * h + fsub
                                pu = PS[5 + fsub]
                                S.wait("tensor", *ktr,
                                       prevc.get(("stup", (fc - 2) % 8)))
                                S.op("tensor", "matmul", out=pu[:, :],
                                     lhsT=A_ktr[:, 0 * KD + fc * 128:
                                               0 * KD + (fc + 1) * 128],
                                     rhs=B["v"][0][:, h * DVH:(h + 1) * DVH],
                                     start=True, stop=False)
                                mu = S.op("tensor", "matmul", out=pu[:, :],
                                          lhsT=A_ktr[:, 1 * KD + fc * 128:
                                                    1 * KD + (fc + 1) * 128],
                                          rhs=B["v"][1][:, h * DVH:(h + 1) * DVH],
                                          start=False, stop=True)
                                prevc[("vlast", cp)] = mu
                                S.wait("vector", mu, ebcr,
                                       prevc.get(("sbcp", fc)))
                                stup = S.op("vector", "scalar_tensor_tensor",
                                            out=A_S[:, fc * DVH:(fc + 1) * DVH],
                                            in0=A_S[:, fc * DVH:(fc + 1) * DVH],
                                            scalar=col(CM_EBC + fc),
                                            in1=pu[:, :], op0=Op.mult, op1=Op.add)
                                prevc[("stup", fc)] = stup
                                S.wait("gpsimd", stup, prevc[("mo", h, 0)],
                                       prevc[("mo", h, 1)])
                                sbcp = S.op("gpsimd", "tensor_copy",
                                            out=A_Sb[:, fc * DVH:(fc + 1) * DVH],
                                            in_=A_S[:, fc * DVH:(fc + 1) * DVH])
                                prevc[("sbcp", fc)] = sbcp
                            prevc["molast"] = prevc[("mo", h, 1)]
                            prevc[("molast", cp)] = prevc[("mo", h, 1)]
                        for th in range(2):
                            S.wait("sync", prevc[("olast", th)], *o_evs)
                            osp = S.dma("sync", f"ol{th}",
                                        out=ol_d[base + th * 128:base + th * 128 + 128, :],
                                        in_=A_ol[th][:, :])
                            prevc[("osp", th)] = osp

                    S.wait("sync", prevc[("stup", 6)], prevc[("stup", 7)],
                           prevc["dacu_last"])
                    S.wait("sync", wag)
                    ccv = cc_sin.ap()[0:KD, :].rearrange("(f p) w -> p f w", p=128)
                    sps = [S.dma("sync", "ccs", out=ccv[:, fc, :],
                                 in_=A_S[:, fc * DVH:(fc + 1) * DVH])
                           for fc in range(NKD)]
                    S.wait("vector", prevc[("ebx", 1)][1])
                    zpad = S.op("vector", "memset", ap=db[1]["b"][:, 0:504],
                                constant=0.0)
                    S.wait("sync", zpad)
                    sps.append(S.dma("sync", "ccs", out=cc_sin.ap()[KD:KD + 128, 0:8],
                                     in_=cols(CM_DAC, 8)))
                    spd = S.dma("sync", "ccs", out=cc_sin.ap()[KD:KD + 128, 8:512],
                                in_=db[1]["b"][:, 0:504])
                    S.wait("gpsimd", *sps, spd)
                    ag = S.cc(lambda g: g.collective_compute(
                        "AllGather", Op.bypass,
                        replica_groups=[[0, 1, 2, 3], [4, 5, 6, 7]],
                        ins=[cc_sin.ap().opt()], outs=[cc_sout.ap().opt()]))
                    on2 = S.op("vector", "memset", ap=A_row[0:1, 40:168],
                               constant=1.0)
                    S.wait("tensor", on2, ld_gm, prevc[("stup", 6)],
                           prevc[("stup", 7)])
                    gmms = [S.op("tensor", "matmul", out=PS[6][0:128, j:j + 1],
                                 lhsT=A_row[0:1, 40:168],
                                 rhs=A_row[0:1, 8 + j:9 + j], start=True,
                                 stop=True) for j in range(4)]
                    S.wait("vector", *gmms)
                    gmc = [S.op("vector", "tensor_copy", out=col(CM_GM + j),
                                in_=PS[6][:, j:j + 1]) for j in range(4)]
                    zin = S.op("vector", "memset", ap=A_Sin[:, :], constant=0.0)
                    zdc = S.op("vector", "memset", ap=cols(CM_T0, 8), constant=1.0)
                    with ExitStack() as p2b:
                        ust = [psb(p2b, f"ust{i}", [128, 512], F32) for i in range(4)]
                        djc = psb(p2b, "djc", [128, 32], F32)
                        S.wait("sync", ag)
                        ROWS = KD + 128
                        ldjs = [S.dma("sync", "cmpd", out=djc[:, j * 8:(j + 1) * 8],
                                      in_=cc_sout.ap()[j * ROWS + KD:
                                                       j * ROWS + KD + 128, 0:8])
                                for j in range(4)]
                        accs = {}
                        for jj in range(4):
                            j = 3 - jj
                            for fc in range(NKD):
                                sl = (jj * NKD + fc) % 4
                                if jj * NKD + fc >= 4:
                                    S.wait("sync", accs[jj * NKD + fc - 4],
                                           accs[("dcu", jj * NKD + fc - 4)])
                                lu = S.dma("sync", f"cmp{sl}", out=ust[sl][:, :],
                                           in_=cc_sout.ap()[j * ROWS + fc * 128:
                                                            j * ROWS + (fc + 1) * 128, 0:512])
                                S.wait("vector", lu, ldjs[-1], zin, zdc,
                                       gmc[j] if jj == 0 and fc == 0 else None)
                                dg = S.op("vector", "tensor_tensor",
                                          out=col(CM_T0 + 8), in0=col(CM_T0 + fc),
                                          in1=col(CM_GM + j), op=Op.mult)
                                S.wait("vector", dg)
                                acc = S.op("vector", "scalar_tensor_tensor",
                                           out=A_Sin[:, fc * DVH:(fc + 1) * DVH],
                                           in0=ust[sl][:, :], scalar=col(CM_T0 + 8),
                                           in1=A_Sin[:, fc * DVH:(fc + 1) * DVH],
                                           op0=Op.mult, op1=Op.add)
                                accs[jj * NKD + fc] = acc
                                t1 = S.op("vector", "tensor_tensor",
                                          out=col(CM_T0 + 9),
                                          in0=djc[:, j * 8 + fc:j * 8 + fc + 1],
                                          in1=col(CM_GM + j), op=Op.mult)
                                t2 = S.op("vector", "tensor_scalar",
                                          out=col(CM_T0 + 10), in0=col(CM_GM + j),
                                          scalar1=-1.0, scalar2=1.0, op0=Op.mult,
                                          op1=Op.add)
                                S.wait("vector", t1, t2)
                                t3 = S.op("vector", "tensor_tensor",
                                          out=col(CM_T0 + 9), in0=col(CM_T0 + 9),
                                          in1=col(CM_T0 + 10), op=Op.add)
                                S.wait("vector", t3)
                                dcu = S.op("vector", "tensor_tensor",
                                           out=col(CM_T0 + fc), in0=col(CM_T0 + fc),
                                           in1=col(CM_T0 + 9), op=Op.mult)
                                accs[("dcu", jj * NKD + fc)] = dcu
                        sinb = []
                        for fc in range(NKD):
                            S.wait("vector", accs[3 * NKD + fc],
                                   prevc[("sbcp", fc)], prevc["molast"])
                            sinb.append(S.op("vector", "tensor_copy",
                                             out=A_Sinb[:, fc * DVH:(fc + 1) * DVH],
                                             in_=A_Sin[:, fc * DVH:(fc + 1) * DVH]))
                S.barrier()

                A_x = stk.enter_context(
                    nc.sbuf_tensor("A_x2", [128, NTT * D], F32, side="right"))
                with ExitStack() as p3:
                    Wof = psb(p3, "Wof", [128, 16 * D], BF16)
                    gnwc = psb(p3, "gnwc", [128, 4], F32)
                    p3a = p3.enter_context(ExitStack())
                    sgb = [psb(p3a, f"A_sg{i}", [128, VD], BF16) for i in range(2)]
                    olb = [psb(p3a, f"A_olr{i}", [128, VD], BF16) for i in range(2)]
                    qgb = [psb(p3a, f"A_qgs{i}", [128, 2 * KD], BF16) for i in range(2)]
                    ofb = [psb(p3a, f"A_of{i}", [128, VD], F32) for i in range(2)]
                    A_sq = psb(p3a, "A_sq", [128, 512], F32)
                    ogb = [psb(p3a, f"A_og{i}", [128, VD], BF16) for i in range(2)]
                    ogt = [psb(p3a, f"A_ogt{i}", [128, VD], BF16) for i in range(2)]

                    lwo = [S.dma("gpsimd", "gw", out=Wof[:, fc * D:(fc + 1) * D],
                                 in_=Wo[fc * 128:(fc + 1) * 128, :])
                           for fc in range(16)]
                    lgn = [S.dma("sync", "gk2", out=gnwc[:, a:a + 1],
                                 in_=gnw[a * 128:(a + 1) * 128][:, None])
                           for a in range(4)]
                    wosc = []
                    for fc in range(16):
                        S.wait("vector", lwo[-1], lgn[-1])
                        wosc.append(S.op("vector", "tensor_scalar",
                                         out=Wof[:, fc * D:(fc + 1) * D],
                                         in0=Wof[:, fc * D:(fc + 1) * D],
                                         scalar1=gnwc[:, fc % 4:fc % 4 + 1],
                                         scalar2=None, op0=Op.mult))

                    qgv2 = qg_d.rearrange("(f p) t -> p f t", p=128)
                    evs = {}
                    prev4 = {}
                    for t in range(NTT):
                        if PARANOID:
                            S.barrier()
                        par = t % 2
                        qpar, qh = (t // 2) % 2, t % 2
                        A_sg, A_olr, A_og = sgb[par], olb[par], ogb[par]
                        A_of = ofb[par]
                        A_qgs = qgb[qpar]
                        if t >= 2:
                            S.wait("sync", prev4[("ogsp", t - 2)],
                                   prev4[("ofl", t - 2)], prev4[("mc", t - 2)])
                        lsg = S.dma("sync", f"c4a{par}", out=A_sg[:, :],
                                    in_=sg_d[t * 128:(t + 1) * 128, :])
                        lol = S.dma("sync", f"c4b{par}", out=A_olr[:, :],
                                    in_=ol_d[t * 128:(t + 1) * 128, :])
                        if t % 2 == 0:
                            if t >= 4:
                                S.wait("sync", prev4[("mc", t - 3)])
                            lqg = S.dma("sync", f"c4c{qpar}",
                                        out=A_qgs.rearrange("p (f t) -> p f t", t=256),
                                        in_=qgv2[:, :, t * 128:(t + 2) * 128])
                            prev4["lqg"] = lqg
                        lqg = prev4["lqg"]
                        lx1 = S.dma("sync", f"c4x{par}",
                                    out=A_x[:, t * D:(t + 1) * D],
                                    in_=x1_d[t * 128:(t + 1) * 128, :])
                        prev4["lx1", t] = lx1
                        prev4["lx1last"] = lx1
                        for h in range(NH):
                            pc_ = PS[h % 2]
                            S.wait("tensor", lqg,
                                   *( [sinb[2 * h], sinb[2 * h + 1]] if t == 0 else []),
                                   prev4.get(("ofp", h % 2)))
                            S.op("tensor", "matmul", out=pc_[:, :],
                                 lhsT=A_qgs[:, (2 * h) * 256 + qh * 128:
                                            (2 * h) * 256 + qh * 128 + 128],
                                 rhs=A_Sinb[:, (2 * h) * DVH:(2 * h + 1) * DVH],
                                 start=True, stop=False)
                            mc = S.op("tensor", "matmul", out=pc_[:, :],
                                      lhsT=A_qgs[:, (2 * h + 1) * 256 + qh * 128:
                                                 (2 * h + 1) * 256 + qh * 128 + 128],
                                      rhs=A_Sinb[:, (2 * h + 1) * DVH:(2 * h + 2) * DVH],
                                      start=False, stop=True)
                            S.wait("vector", mc, lol,
                                   prev4.get(("sq", h, par)),
                                   prev4.get(("og", h, par)))
                            ofadd = S.op("vector", "tensor_tensor",
                                         out=A_of[:, h * DVH:(h + 1) * DVH],
                                         in0=PS[h % 2][:, :],
                                         in1=A_olr[:, h * DVH:(h + 1) * DVH],
                                         op=Op.add)
                            prev4[("ofadd", h)] = ofadd
                            prev4[("ofp", h % 2)] = ofadd
                            S.wait("scalar", ofadd,
                                   prev4.get(("sq", h - 1, par)))
                            sqr = S.op("scalar", "activation", out=A_sq[:, :],
                                       in_=A_of[:, h * DVH:(h + 1) * DVH],
                                       func=AF.Square,
                                       accum_out=col(CM_T0 + 12 + par * 4 + h))
                            prev4[("sq", h, par)] = sqr
                        S.wait("scalar", prev4[("sq", 3, par)])
                        rms = S.op("scalar", "activation",
                                   out=cols(CM_RMS + par * 4, 4),
                                   in_=cols(CM_T0 + 12 + par * 4, 4), func=AF.Sqrt,
                                   scale=1.0 / DVH, bias=EPS)
                        S.wait("vector", rms)
                        rmr = S.op("vector", "reciprocal",
                                   out=cols(CM_RMS + par * 4, 4),
                                   in_=cols(CM_RMS + par * 4, 4))
                        oglast = None
                        for h in range(NH):
                            S.wait("vector", rmr, lsg,
                                   *( [prev4[("ogsp", t - 2)]] if (h == 0 and t >= 2) else []))
                            og = S.op("vector", "scalar_tensor_tensor",
                                      out=A_og[:, h * DVH:(h + 1) * DVH],
                                      in0=A_of[:, h * DVH:(h + 1) * DVH],
                                      scalar=col(CM_RMS + par * 4 + h),
                                      in1=A_sg[:, h * DVH:(h + 1) * DVH],
                                      op0=Op.mult, op1=Op.mult)
                            prev4[("og", h, par)] = og
                            oglast = og
                        A_ogt = ogt[par]
                        evlast = None
                        for b2 in range(2):
                            trs = None
                            for i in range(8):
                                fc16 = b2 * 8 + i
                                if i == 0:
                                    S.wait("tensor", oglast, id1,
                                           prev4.get(("pbt", 1 - b2 if b2 else 1)),
                                           *( [prev4.get(("womd", t - 2))]
                                              if (b2 == 0 and t >= 2) else []))
                                trs = S.op("tensor", "transpose",
                                           out=PBT[:, i * 128:(i + 1) * 128],
                                           in_=A_og[:, fc16 * 128:(fc16 + 1) * 128],
                                           identity=A_id[:, :])
                            S.wait("vector", trs)
                            pbe = S.op("vector", "tensor_copy",
                                       out=A_ogt[:, b2 * 1024:(b2 + 1) * 1024],
                                       in_=PBT[:, :])
                            prev4[("pbt", b2)] = pbe
                        pw = (PS[2], PS[3]) if par == 0 else (PS[4], PS[5])
                        S.wait("tensor", prev4[("pbt", 0)], prev4[("pbt", 1)],
                               lwo[-1] if t == 0 else None,
                               *( [prev4.get(("woev", t - 2))] if t >= 2 else []))
                        if t == 0:
                            S.wait("tensor", *wosc)
                        md = None
                        for fc16 in range(16):
                            S.op("tensor", "matmul", out=pw[0][:, :],
                                 lhsT=A_ogt[:, fc16 * 128:(fc16 + 1) * 128],
                                 rhs=Wof[:, fc16 * D:fc16 * D + 512],
                                 start=(fc16 == 0), stop=(fc16 == 15))
                            md = S.op("tensor", "matmul", out=pw[1][:, :],
                                      lhsT=A_ogt[:, fc16 * 128:(fc16 + 1) * 128],
                                      rhs=Wof[:, fc16 * D + 512:(fc16 + 1) * D],
                                      start=(fc16 == 0), stop=(fc16 == 15))
                        prev4[("womd", t)] = md
                        S.wait("vector", md, prev4[("lx1", t)])
                        S.op("vector", "tensor_tensor",
                             out=A_x[:, t * D:t * D + 512],
                             in0=A_x[:, t * D:t * D + 512],
                             in1=pw[0][:, :], op=Op.add)
                        woev = S.op("vector", "tensor_tensor",
                                    out=A_x[:, t * D + 512:(t + 1) * D],
                                    in0=A_x[:, t * D + 512:(t + 1) * D],
                                    in1=pw[1][:, :], op=Op.add)
                        prev4[("woev", t)] = woev
                        evs[(1, t)] = woev
                        prev4[("ogsp", t)] = woev
                        prev4[("ofl", t)] = prev4[("ofadd", 3)]
                        prev4[("mc", t)] = mc

                    p3a.close()
            S.barrier()
            S.barrier()
            return evs

        S.wait("vector", *ld_x)
        S.wait("scalar", *ld_x)
        ffn1 = emit_ffn(True, 0, out_final=(STAGE == 1))
        if STAGE == 1:
            lasts = [ffn1[("out", t)] for t in range(NTT)]
            if DEBUG:
                S.wait("gpsimd", *lasts)
                lasts = [
                    S.dma("gpsimd", "dbg", out=d_xq, in_=xq_d),
                    S.dma("gpsimd", "dbg", out=d_h, in_=h_d[0:256, :]),
                    S.dma("gpsimd", "dbg", out=d_wg, in_=wdg_bf[0:256, :]),
                    S.dma("gpsimd", "dbg", out=d_cols, in_=A_cols[:, :]),
                    S.dma("gpsimd", "dbg", out=d_row, in_=A_row[:, :]),
                ]
                S.wait("gpsimd", *lasts)
            S.wait("sync", *lasts)
            S.finalize(nc, block, stk)
            return nc

        S.barrier()
        attn_evs = emit_attn()
        if STAGE == 2:
            finals = []
            for t in range(NTT):
                S.wait("sync", attn_evs[(1, t)])
                finals.append(S.dma("sync", "out", out=out[t * 128:(t + 1) * 128, :],
                                    in_=A_x[:, t * D:(t + 1) * D]))
            if DEBUG2:
                S.wait("gpsimd", *finals)
                finals = [
                    S.dma("gpsimd", "dbg", out=d_q, in_=q_d),
                    S.dma("gpsimd", "dbg", out=d_k, in_=k_d),
                    S.dma("gpsimd", "dbg", out=d_g, in_=g_d),
                    S.dma("gpsimd", "dbg", out=d_v, in_=v_d),
                    S.dma("gpsimd", "dbg", out=d_sg, in_=sg_d),
                    S.dma("gpsimd", "dbg", out=d_ol, in_=ol_d),
                    S.dma("gpsimd", "dbg", out=d_qg, in_=qg_d),
                    S.dma("gpsimd", "dbg", out=d_og, in_=og_d),
                    S.dma("gpsimd", "dbg", out=d_xn1, in_=xn1_d),
                    S.dma("gpsimd", "dbg", out=d_ccin, in_=cc_sin.ap()),
                    S.dma("gpsimd", "dbg", out=d_ccout, in_=cc_sout.ap()),
                ]
                S.wait("gpsimd", *finals)
            S.wait("sync", *finals)
            S.finalize(nc, block, stk)
            return nc
        ffn2 = emit_ffn(False, 2, out_final=True)
        S.wait("sync", *[ffn2[("out", t)] for t in range(NTT)])
        S.finalize(nc, block, stk)
    return nc


_prog_cache = {}


def _get_prog():
    if "nc" not in _prog_cache:
        _prog_cache["nc"] = build_program()
    return _prog_cache["nc"]


def kernel(L0_attn_x, L0_attn_Wq, L0_attn_Wk, L0_attn_Wv, L0_attn_Wg,
           L0_attn_gk_w1, L0_attn_gk_w2, L0_attn_gk_b2, L0_attn_g_norm_w,
           L0_attn_Wo, L0_ffn_clip_x, L0_ffn_clip_down, L0_ffn_w_gate,
           L0_ffn_w_up, L0_ffn_w_down):
    x = np.ascontiguousarray(np.asarray(L0_attn_x, dtype=np.float32))
    full = dict(
        Wq=np.ascontiguousarray(L0_attn_Wq, dtype=np.float32),
        Wk=np.ascontiguousarray(L0_attn_Wk, dtype=np.float32),
        Wv=np.ascontiguousarray(L0_attn_Wv, dtype=np.float32),
        Wg=np.ascontiguousarray(L0_attn_Wg, dtype=np.float32),
        gk1=np.ascontiguousarray(L0_attn_gk_w1, dtype=np.float32),
        gk2=np.ascontiguousarray(L0_attn_gk_w2, dtype=np.float32),
        gkb=np.ascontiguousarray(L0_attn_gk_b2, dtype=np.float32),
        gnw=np.ascontiguousarray(L0_attn_g_norm_w, dtype=np.float32),
        Wo=np.ascontiguousarray(L0_attn_Wo, dtype=np.float32),
        clipx=np.ascontiguousarray(L0_ffn_clip_x, dtype=np.float32),
        clipd=np.ascontiguousarray(L0_ffn_clip_down, dtype=np.float32),
        wga=np.ascontiguousarray(L0_ffn_w_gate, dtype=np.float32),
        wup=np.ascontiguousarray(L0_ffn_w_up, dtype=np.float32),
        wdo=np.ascontiguousarray(L0_ffn_w_down, dtype=np.float32),
    )
    in_maps = []
    for i in range(8):
        b, p = i // 4, i % 4
        m = dict(full)
        m["xs"] = np.ascontiguousarray(x[b, p * TOK:(p + 1) * TOK, :])
        gm = np.zeros(4, np.float32)
        gm[:p] = 1.0
        m["gmask"] = gm
        in_maps.append(m)
    nc = _get_prog()
    res = run_bass_kernel_spmd(nc, in_maps, core_ids=list(range(8)))
    outp = np.empty((2, 8192, D), np.float32)
    for i in range(8):
        b, p = i // 4, i % 4
        outp[b, p * TOK:(p + 1) * TOK, :] = res.results[i]["out"]
    return outp


# version 08 (final)
# speedup vs baseline: 1.0583x; 1.0583x over previous
import numpy as np
import ml_dtypes
from contextlib import ExitStack

import concourse.bass as bass
import concourse.mybir as mybir
from concourse.bass_utils import run_bass_kernel_spmd
from concourse.alu_op_type import AluOpType as Op

F32 = mybir.dt.float32
BF16 = mybir.dt.bfloat16
I8 = mybir.dt.int8
AF = mybir.ActivationFunctionType
AX = mybir.AxisListType

TOK = 2048
D = 1024
IFF = 2816
NH, DKH, DVH = 4, 256, 512
KD, VD = 1024, 2048
CHK = 256
NCH = TOK // CHK
NTT = TOK // 128
NKD = D // 128
NIC = IFF // 128
EPS = 1e-5
QMAX = 127.0

STAGE = 3
PARANOID = False
DEBUG = False
DEBUG2 = False

ENGINES = ("sync", "vector", "scalar", "tensor", "gpsimd")


class _Rec:
    __slots__ = ("eng", "fn", "kind", "need", "val", "ch", "seq")

    def __init__(self, eng, fn, kind, ch=None, seq=0):
        self.eng, self.fn, self.kind = eng, fn, kind
        self.need, self.val = False, None
        self.ch = ch
        self.seq = seq


class Sched:
    def __init__(self):
        self.items = {e: [] for e in ENGINES}
        self.cc_recs = []
        self.last = {e: None for e in ENGINES}
        self.chlast = {}

    def op(self, eng, fname, **kw):
        def fn(e, fname=fname, kw=kw):
            return getattr(e, fname)(**kw)
        r = _Rec(eng, fn, "c")
        self.items[eng].append(r)
        self.last[eng] = r
        return r

    def barrier(self):
        lasts = list(self.last.items()) + list(self.chlast.items())
        for e in ENGINES:
            for o, r in lasts:
                if r is not None and not (o == e and r.kind == "c"):
                    self.wait(e, r)

    def dma(self, eng, ch, out, in_, **kw):
        def fn(e, out=out, in_=in_, kw=kw):
            return e.dma_start(out=out, in_=in_, **kw)
        seq = self.chlast[ch].seq + 1 if ch in self.chlast else 1
        r = _Rec(eng, fn, "d", ch, seq)
        self.items[eng].append(r)
        self.chlast[ch] = r
        return r

    def dma_tr(self, eng, ch, out, in_):
        def fn(e, out=out, in_=in_):
            return e.dma_start_transpose(out=out, in_=in_)
        seq = self.chlast[ch].seq + 1 if ch in self.chlast else 1
        r = _Rec(eng, fn, "d", ch, seq)
        self.items[eng].append(r)
        self.chlast[ch] = r
        return r

    def cc(self, fn):
        r = _Rec("gpsimd", fn, "cc")
        self.items["gpsimd"].append(r)
        self.cc_recs.append(r)
        self.last["gpsimd"] = r
        return r

    def wait(self, eng, *recs):
        bych = {}
        rest = []
        for r in recs:
            if r is None:
                continue
            if r.kind == "d":
                k = r.ch
                if k not in bych or bych[k].seq < r.seq:
                    bych[k] = r
            else:
                rest.append(r)
        for r in list(bych.values()) + rest:
            r.need = True
            self.items[eng].append(("w", r))

    def finalize(self, nc, block, stack):
        sems = {e: stack.enter_context(nc.semaphore(f"se_{e}")) for e in ENGINES}
        chs = sorted(self.chlast)
        chsems = {ch: stack.enter_context(nc.semaphore(f"sc_{ch}")) for ch in chs}
        cc_sems = [stack.enter_context(nc.semaphore(f"scc{i}"))
                   for i in range(len(self.cc_recs))]
        cc_map = {id(r): s for r, s in zip(self.cc_recs, cc_sems)}
        chcnt = {ch: 0 for ch in chs}
        for e in ENGINES:
            c = 0
            for it in self.items[e]:
                if not isinstance(it, _Rec):
                    continue
                if it.kind == "d":
                    it.need = True
                    chcnt[it.ch] += 16
                    it.val = chcnt[it.ch]
                elif it.kind == "c" and it.need:
                    c += 1
                    it.val = c

        def mk(e):
            def emit(engine_obj):
                for it in self.items[e]:
                    if isinstance(it, _Rec):
                        inst = it.fn(engine_obj)
                        if it.need:
                            if it.kind == "cc":
                                inst.then_inc(cc_map[id(it)])
                            elif it.kind == "d":
                                inst.then_inc(chsems[it.ch], 16)
                            else:
                                inst.then_inc(sems[e], 1)
                    else:
                        _, r = it
                        if r.kind == "cc":
                            engine_obj.wait_ge(cc_map[id(r)], 1)
                        elif r.kind == "d":
                            engine_obj.wait_ge(chsems[r.ch], r.val)
                        else:
                            engine_obj.wait_ge(sems[r.eng], r.val)
            return emit

        block.sync(mk("sync"))
        block.vector(mk("vector"))
        block.scalar(mk("scalar"))
        block.tensor(mk("tensor"))
        block.gpsimd(mk("gpsimd"))


def build_program():
    nc = bass.Bass("TRN2", target_bir_lowering=False, debug=False)

    xs = nc.dram_tensor("xs", [TOK, D], F32, kind="ExternalInput").ap()
    Wq = nc.dram_tensor("Wq", [D, KD], F32, kind="ExternalInput").ap()
    Wk = nc.dram_tensor("Wk", [D, KD], F32, kind="ExternalInput").ap()
    Wv = nc.dram_tensor("Wv", [D, VD], F32, kind="ExternalInput").ap()
    Wg = nc.dram_tensor("Wg", [D, VD], F32, kind="ExternalInput").ap()
    gk1 = nc.dram_tensor("gk1", [D, 16], F32, kind="ExternalInput").ap()
    gk2 = nc.dram_tensor("gk2", [16, KD], F32, kind="ExternalInput").ap()
    gkb = nc.dram_tensor("gkb", [KD], F32, kind="ExternalInput").ap()
    gnw = nc.dram_tensor("gnw", [DVH], F32, kind="ExternalInput").ap()
    Wo = nc.dram_tensor("Wo", [VD, D], F32, kind="ExternalInput").ap()
    clipx = nc.dram_tensor("clipx", [1], F32, kind="ExternalInput").ap()
    clipd = nc.dram_tensor("clipd", [1], F32, kind="ExternalInput").ap()
    wga = nc.dram_tensor("wga", [IFF, D], F32, kind="ExternalInput").ap()
    wup = nc.dram_tensor("wup", [IFF, D], F32, kind="ExternalInput").ap()
    wdo = nc.dram_tensor("wdo", [D, IFF], F32, kind="ExternalInput").ap()
    gmask = nc.dram_tensor("gmask", [4], F32, kind="ExternalInput").ap()
    out = nc.dram_tensor("out", [TOK, D], F32, kind="ExternalOutput").ap()
    if DEBUG:
        d_xq = nc.dram_tensor("d_xq", [TOK, D], F32, kind="ExternalOutput").ap()
        d_h = nc.dram_tensor("d_h", [256, TOK], F32, kind="ExternalOutput").ap()
        d_wg = nc.dram_tensor("d_wg", [256, D], F32, kind="ExternalOutput").ap()
        d_cols = nc.dram_tensor("d_cols", [128, 168], F32, kind="ExternalOutput").ap()
        d_row = nc.dram_tensor("d_row", [1, 512], F32, kind="ExternalOutput").ap()
    if DEBUG2:
        d_q = nc.dram_tensor("d_q", [KD, TOK], F32, kind="ExternalOutput").ap()
        d_k = nc.dram_tensor("d_k", [KD, TOK], F32, kind="ExternalOutput").ap()
        d_g = nc.dram_tensor("d_g", [KD, TOK], F32, kind="ExternalOutput").ap()
        d_v = nc.dram_tensor("d_v", [TOK, VD], F32, kind="ExternalOutput").ap()
        d_sg = nc.dram_tensor("d_sg", [TOK, VD], F32, kind="ExternalOutput").ap()
        d_ol = nc.dram_tensor("d_ol", [TOK, VD], F32, kind="ExternalOutput").ap()
        d_qg = nc.dram_tensor("d_qg", [KD, TOK], F32, kind="ExternalOutput").ap()
        d_og = nc.dram_tensor("d_og", [TOK, VD], F32, kind="ExternalOutput").ap()
        d_xn1 = nc.dram_tensor("d_xn1", [TOK, D], F32, kind="ExternalOutput").ap()
        d_ccin = nc.dram_tensor("d_ccin", [KD, 513], F32, kind="ExternalOutput").ap()
        d_ccout = nc.dram_tensor("d_ccout", [4 * KD, 513], F32, kind="ExternalOutput").ap()

    wdg_bf = nc.dram_tensor("wdg_bf", [IFF, D], BF16).ap()
    wdu_bf = nc.dram_tensor("wdu_bf", [IFF, D], BF16).ap()
    wdd_bf = nc.dram_tensor("wdd_bf", [D, IFF], BF16).ap()
    xq_d = nc.dram_tensor("xq_d", [TOK, D], BF16).ap()
    h_d = nc.dram_tensor("h_d", [IFF, TOK], BF16).ap()
    xn1_d = nc.dram_tensor("xn1_d", [TOK, D], BF16).ap()
    x1_d = nc.dram_tensor("x1_d", [TOK, D], F32).ap()
    qg_d = nc.dram_tensor("qg_d", [KD, TOK], BF16).ap()
    q_d = nc.dram_tensor("q_d", [KD, TOK], BF16).ap()
    k_d = nc.dram_tensor("k_d", [KD, TOK], BF16).ap()
    g_d = nc.dram_tensor("g_d", [KD, TOK], BF16).ap()
    v_d = nc.dram_tensor("v_d", [TOK, VD], BF16).ap()
    sg_d = nc.dram_tensor("sg_d", [TOK, VD], BF16).ap()
    ol_d = nc.dram_tensor("ol_d", [TOK, VD], BF16).ap()
    og_d = nc.dram_tensor("og_d", [TOK, VD], BF16).ap()
    kh_d = nc.dram_tensor("kh_d", [KD, CHK], BF16).ap()
    cc_sin = nc.dram_tensor("cc_sin", [KD + 128, 512], F32)
    cc_sout = nc.dram_tensor("cc_sout", [4 * (KD + 128), 512], F32)
    mx_in = [nc.dram_tensor(f"mx_in{i}", [1, 8], F32) for i in range(4)]
    mx_out = [nc.dram_tensor(f"mx_out{i}", [1, 8], F32) for i in range(4)]

    S = Sched()

    with ExitStack() as stk:
        def sb(name, shape, dt):
            return stk.enter_context(nc.sbuf_tensor(name, shape, dt))

        def psb(ps, name, shape, dt):
            return ps.enter_context(nc.sbuf_tensor(name, shape, dt))

        cmX1 = nc.sbuf_tensor("A_x1", [128, NTT * D], F32, side="right")
        A_x = cmX1.__enter__()
        A_cols = sb("A_cols", [128, 168], F32)
        A_row = sb("A_row", [1, 512], F32)
        A_i8 = sb("A_i8", [128, IFF], I8)
        A_const = sb("A_const", [128, 2], F32)
        nc.const_aps.register(F32, 1.0, A_const[:, 0:1])
        nc.const_aps.register(F32, EPS, A_const[:, 1:2])
        PS = [stk.enter_context(nc.psum_tensor(f"ps{i}", [128, 512], F32))
              for i in range(7)]
        PBT = stk.enter_context(nc.psum_tensor("pbt", [128, 1024], BF16))
        block = stk.enter_context(nc.Block())

        CM_M, CM_R, CM_XMX, CM_XMN, CM_AM = 0, 16, 32, 48, 64
        CM_HMX, CM_SX, CM_ISX, CM_SH, CM_ISH = 80, 81, 82, 83, 84
        CM_T0 = 85
        CM_GM = 105
        CM_RMS = 109
        CM_GNW = 113
        CM_DAC = 117
        CM_EBC = 125
        CM_EC = 133
        CM_EM = 141
        CM_NM = 149
        CM_PM = 157

        col = lambda i: A_cols[:, i:i + 1]
        cols = lambda i, n: A_cols[:, i:i + n]

        cst1 = S.op("vector", "memset", ap=A_const[:, 0:1], constant=1.0)
        cst2 = S.op("vector", "memset", ap=A_const[:, 1:2], constant=EPS)
        S.wait("scalar", cst1, cst2)
        dz = sb("dz", [128, 512], F32)
        zdz = S.op("vector", "memset", ap=dz[:, :], constant=0.0)
        S.wait("sync", zdz)
        wz = [S.dma("sync", "warm", out=mx_in[0].ap(), in_=dz[0:1, 0:8])]
        for rr in range(9):
            wz.append(S.dma("sync", "warm",
                            out=cc_sin.ap()[rr * 128:(rr + 1) * 128, :],
                            in_=dz[:, :]))
        S.wait("gpsimd", *wz)
        war = S.cc(lambda g: g.collective_compute(
            "AllReduce", Op.max, replica_groups=[list(range(8))],
            ins=[mx_in[0].ap().opt()], outs=[mx_out[0].ap().opt()]))
        wag = S.cc(lambda g: g.collective_compute(
            "AllGather", Op.bypass, replica_groups=[[0, 1, 2, 3], [4, 5, 6, 7]],
            ins=[cc_sin.ap().opt()], outs=[cc_sout.ap().opt()]))
        ld_x = [S.dma("sync", "ldx", out=A_x[:, t * D:(t + 1) * D],
                      in_=xs[t * 128:(t + 1) * 128, :]) for t in range(NTT)]
        ld_clip = S.dma("sync", "ldc", out=A_row[0:1, 0:1], in_=clipx[None, :])
        ld_clipd = S.dma("sync", "ldc", out=A_row[0:1, 1:2], in_=clipd[None, :])
        ld_gm = S.dma("sync", "ldc", out=A_row[0:1, 8:12], in_=gmask[None, :])

        def emit_weight_quant(ps):
            sp_gu, sp_d = [], []
            with ExitStack() as wps:
                NWB = 3
                wt = [psb(wps, f"wqt{i}", [128, D], F32) for i in range(NWB)]
                wb = [psb(wps, f"wqb{i}", [128, D], BF16) for i in range(NWB)]
                hist = []
                for src, dst in ((wga, wdg_bf), (wup, wdu_bf)):
                    for tt in range(IFF // 128):
                        b = len(hist) % NWB
                        tc = CM_T0 + 3 * (len(hist) % 2)
                        if len(hist) >= NWB:
                            S.wait("gpsimd", hist[-NWB]["st"])
                            S.wait("scalar", hist[-NWB]["st"])
                        if len(hist) >= 2:
                            S.wait("vector", hist[-2]["a2"])
                        ld = S.dma("gpsimd", f"wq{b}", out=wt[b][:, :],
                                   in_=src[tt * 128:(tt + 1) * 128, :])
                        S.wait("vector", ld)
                        rm = S.op("vector", "tensor_reduce", out=col(tc),
                                  in_=wt[b][:, :], axis=AX.X, op=Op.max,
                                  apply_absolute_value=True)
                        S.wait("vector", rm)
                        sc = S.op("vector", "tensor_scalar", out=col(tc + 1),
                                  in0=col(tc), scalar1=1.0 / QMAX, scalar2=None,
                                  op0=Op.mult)
                        S.wait("vector", sc)
                        iv = S.op("vector", "reciprocal", out=col(tc + 2),
                                  in_=col(tc + 1))
                        S.wait("scalar", iv, sc,
                               *( [hist[-1]["a2"]] if hist else []))
                        a1 = S.op("scalar", "activation", out=A_i8[:, 0:D],
                                  in_=wt[b][:, :], func=AF.Copy, scale=col(tc + 2))
                        S.wait("scalar", a1)
                        a2 = S.op("scalar", "activation", out=wb[b][:, :],
                                  in_=A_i8[:, 0:D], func=AF.Copy, scale=col(tc + 1))
                        S.wait("sync", a2)
                        st = S.dma("sync", f"ws{len(hist) % NWB}", out=dst[tt * 128:(tt + 1) * 128, :],
                                   in_=wb[b][:, :])
                        hist.append(dict(st=st, a2=a2))
                        sp_gu.append(st)
                guard = list(hist[-3:])
            with ExitStack() as wps:
                wt = psb(wps, "wqtd", [128, IFF], F32)
                wb = psb(wps, "wqbd", [128, IFF], BF16)
                prev = None
                for tt in range(D // 128):
                    tc = CM_T0
                    if prev is not None:
                        S.wait("gpsimd", prev["st"])
                        S.wait("vector", prev["a2"])
                        S.wait("scalar", prev["st"])
                    else:
                        S.wait("gpsimd", guard[-1]["a2"], guard[-1]["st"],
                               guard[-2]["st"], guard[-3]["st"])
                        S.wait("scalar", guard[-1]["st"], guard[-2]["st"],
                               guard[-3]["st"])
                        S.wait("vector", guard[-1]["a2"])
                    ld = S.dma("gpsimd", "wq0", out=wt[:, :],
                               in_=wdo[tt * 128:(tt + 1) * 128, :])
                    S.wait("vector", ld)
                    rm = S.op("vector", "tensor_reduce", out=col(tc), in_=wt[:, :],
                              axis=AX.X, op=Op.max, apply_absolute_value=True)
                    S.wait("vector", rm)
                    ts1 = S.op("vector", "tensor_scalar", out=col(tc + 1),
                               in0=col(tc), scalar1=1.0 / QMAX, scalar2=None,
                               op0=Op.mult)
                    S.wait("vector", ts1)
                    iv = S.op("vector", "reciprocal", out=col(tc + 2),
                              in_=col(tc + 1))
                    sc2 = S.op("vector", "tensor_scalar", out=col(tc + 1),
                               in0=col(tc + 1), scalar1=0.5, scalar2=None,
                               op0=Op.mult)
                    S.wait("scalar", iv, sc2,
                           *( [prev["a2"]] if prev else [guard[-1]["a2"]]))
                    a1 = S.op("scalar", "activation", out=A_i8[:, 0:IFF],
                              in_=wt[:, :], func=AF.Copy, scale=col(tc + 2))
                    S.wait("scalar", a1)
                    a2 = S.op("scalar", "activation", out=wb[:, :],
                              in_=A_i8[:, 0:IFF], func=AF.Copy, scale=col(tc + 1))
                    S.wait("sync", a2)
                    st = S.dma("sync", "ws0", out=wdd_bf[tt * 128:(tt + 1) * 128, :],
                               in_=wb[:, :])
                    prev = dict(st=st, a2=a2)
                    sp_d.append(st)
            return sp_gu, sp_d

        def emit_ln_stats(scr, want_minmax):
            recs = []
            vv_hist = [None, None]
            for t in range(NTT):
                pa, pb, pc = CM_T0 + 6 * (t % 2), CM_T0 + 6 * (t % 2) + 1, \
                    CM_T0 + 6 * (t % 2) + 2
                xt = A_x[:, t * D:(t + 1) * D]
                S.wait("scalar", vv_hist[t % 2])
                s1 = S.op("scalar", "activation", out=scr[:, :], in_=xt,
                          func=AF.Copy, accum_out=col(pa))
                S.wait("scalar", s1)
                s2 = S.op("scalar", "activation", out=scr[:, :], in_=xt,
                          func=AF.Square, accum_out=col(pb))
                S.wait("vector", s1, s2)
                mm = S.op("vector", "tensor_scalar", out=col(CM_M + t),
                          in0=col(pa), scalar1=1.0 / D, scalar2=None, op0=Op.mult)
                S.wait("vector", mm)
                t2 = S.op("vector", "tensor_tensor", out=col(pc),
                          in0=col(CM_M + t), in1=col(CM_M + t), op=Op.mult)
                S.wait("vector", t2)
                vv = S.op("vector", "scalar_tensor_tensor", out=col(pc),
                          in0=col(pb), scalar=1.0 / D, in1=col(pc),
                          op0=Op.mult, op1=Op.subtract)
                S.wait("scalar", vv)
                sq = S.op("scalar", "activation", out=col(pc), in_=col(pc),
                          func=AF.Sqrt, bias=EPS)
                vv_hist[t % 2] = sq
                S.wait("vector", sq)
                rr = S.op("vector", "reciprocal", out=col(CM_R + t),
                          in_=col(pc))
                if want_minmax:
                    S.op("vector", "tensor_reduce", out=col(CM_XMX + t), in_=xt,
                         axis=AX.X, op=Op.max)
                    rr = S.op("vector", "tensor_reduce", out=col(CM_XMN + t),
                              in_=xt, axis=AX.X, op=Op.min)
                recs.append(rr)
            return recs

        def emit_armax(cc_idx, src_ap, dep_recs):
            z = S.op("vector", "memset", ap=A_row[0:1, 16:24], constant=0.0)
            S.wait("gpsimd", z, *dep_recs)
            pr = S.op("gpsimd", "tensor_reduce", out=A_row[0:1, 16:17],
                      in_=src_ap, axis=AX.XYZWC, op=Op.max)
            S.wait("gpsimd", pr, war)
            d1 = S.dma("gpsimd", "mx", out=mx_in[cc_idx].ap(), in_=A_row[0:1, 16:24])
            S.wait("gpsimd", d1)
            ar = S.cc(lambda g, i=cc_idx: g.collective_compute(
                "AllReduce", Op.max, replica_groups=[list(range(8))],
                ins=[mx_in[i].ap().opt()], outs=[mx_out[i].ap().opt()]))
            S.wait("gpsimd", ar)
            return S.dma("gpsimd", "mx", out=A_row[0:1, 24:32], in_=mx_out[cc_idx].ap())

        def emit_gmax_cols():
            last = None
            for t in range(NTT):
                S.wait("vector", last)
                S.op("vector", "tensor_tensor", out=col(CM_T0),
                     in0=col(CM_XMX + t), in1=col(CM_M + t), op=Op.subtract)
                u2 = S.op("vector", "tensor_tensor", out=col(CM_T0 + 1),
                          in0=col(CM_M + t), in1=col(CM_XMN + t), op=Op.subtract)
                S.wait("vector", u2)
                u3 = S.op("vector", "tensor_tensor", out=col(CM_T0),
                          in0=col(CM_T0), in1=col(CM_T0 + 1), op=Op.max)
                S.wait("vector", u3)
                last = S.op("vector", "tensor_tensor", out=col(CM_AM + t),
                            in0=col(CM_T0), in1=col(CM_R + t), op=Op.mult)
            return last

        def emit_scale_cols(gmax_rec, clip_off, sx_col, isx_col):
            S.wait("scalar", gmax_rec, ld_clip, ld_clipd, ld_gm)
            sg = S.op("scalar", "activation", out=A_row[0:1, 32:33],
                      in_=A_row[0:1, clip_off:clip_off + 1], func=AF.Sigmoid)
            S.wait("vector", sg, gmax_rec)
            w1 = S.op("vector", "tensor_tensor", out=A_row[0:1, 33:34],
                      in0=A_row[0:1, 32:33], in1=A_row[0:1, 24:25], op=Op.mult)
            S.wait("vector", w1)
            sm = S.op("vector", "tensor_scalar", out=A_row[0:1, 34:35],
                      in0=A_row[0:1, 33:34], scalar1=1.0 / QMAX, scalar2=None,
                      op0=Op.mult)
            S.wait("vector", sm)
            iv = S.op("vector", "reciprocal", out=A_row[0:1, 35:36],
                      in_=A_row[0:1, 34:35])
            on = S.op("vector", "memset", ap=A_row[0:1, 40:168], constant=1.0)
            S.wait("tensor", sm, iv, on)
            m1 = S.op("tensor", "matmul", out=PS[6][0:128, 0:1],
                      lhsT=A_row[0:1, 40:168], rhs=A_row[0:1, 34:35],
                      start=True, stop=True)
            m2 = S.op("tensor", "matmul", out=PS[6][0:128, 1:2],
                      lhsT=A_row[0:1, 40:168], rhs=A_row[0:1, 35:36],
                      start=True, stop=True)
            S.wait("vector", m1, m2)
            c1 = S.op("vector", "tensor_copy", out=col(sx_col), in_=PS[6][:, 0:1])
            c2 = S.op("vector", "tensor_copy", out=col(isx_col), in_=PS[6][:, 1:2])
            return c1, c2

        def emit_ffn(first_call, cc0, out_final):
            with ExitStack() as ps:
                wgf = psb(ps, f"wgf{cc0}", [128, NKD * IFF], BF16)
                wuf = psb(ps, f"wuf{cc0}", [128, NKD * IFF], BF16)
                xqf = psb(ps, f"xqf{cc0}", [128, NKD * 1024], BF16)
                tmp = psb(ps, f"ftmp{cc0}", [128, D], F32)
                xqb = psb(ps, f"fxqb{cc0}", [128, D], BF16)
                hbuf = [psb(ps, f"hb{cc0}_{i}", [128, 1024], BF16) for i in range(2)]
                silu = psb(ps, f"silu{cc0}", [128, 512], F32)

                ln_recs = emit_ln_stats(tmp, True)
                am_last = emit_gmax_cols()
                gm = emit_armax(cc0, cols(CM_AM, NTT), [am_last])
                c1, c2 = emit_scale_cols(gm, 0, CM_SX, CM_ISX)

                if first_call:
                    sp_gu, sp_d = emit_weight_quant(ps)
                else:
                    sp_gu, sp_d = [], []

                if first_call:
                    S.wait("sync", *sp_gu)
                trw = []
                for kk in range(NKD):
                    trw.append(S.dma_tr("sync", "trw", out=wgf[:, kk * IFF:(kk + 1) * IFF],
                                        in_=wdg_bf[:, kk * 128:(kk + 1) * 128]))
                    trw.append(S.dma_tr("sync", "trw", out=wuf[:, kk * IFF:(kk + 1) * IFF],
                                        in_=wdu_bf[:, kk * 128:(kk + 1) * 128]))

                S.wait("scalar", c1, c2)
                sp_xq = []
                prev_a1 = None
                for t in range(NTT):
                    xt = A_x[:, t * D:(t + 1) * D]
                    S.wait("vector", prev_a1)
                    q1 = S.op("vector", "tensor_scalar", out=tmp[:, :], in0=xt,
                              scalar1=col(CM_M + t), scalar2=col(CM_R + t),
                              op0=Op.subtract, op1=Op.mult)
                    S.wait("scalar", q1,
                           *( [sp_d[-1]] if (t == 0 and sp_d) else []))
                    if t >= 1:
                        S.wait("scalar", sp_xq[t - 1])
                    a1 = S.op("scalar", "activation", out=A_i8[:, 0:D],
                              in_=tmp[:, :], func=AF.Copy, scale=col(CM_ISX))
                    prev_a1 = a1
                    S.wait("scalar", a1)
                    a2 = S.op("scalar", "activation", out=xqb[:, :],
                              in_=A_i8[:, 0:D], func=AF.Copy, scale=col(CM_SX))
                    S.wait("sync", a2)
                    sp_xq.append(S.dma("sync", "xqs", out=xq_d[t * 128:(t + 1) * 128, :],
                                       in_=xqb[:, :]))

                hm0 = S.op("vector", "memset", ap=col(CM_HMX), constant=0.0)
                ev = {}
                mm_last = None
                sl_prev = None
                hx_last = None
                g = 0
                for half in range(2):
                    S.wait("sync", *sp_xq)
                    if half == 1:
                        S.wait("sync", mm_last)
                    trx = []
                    for kk in range(NKD):
                        trx.append(S.dma_tr(
                            "sync", "trx", out=xqf[:, kk * 1024:(kk + 1) * 1024],
                            in_=xq_d[half * 1024:(half + 1) * 1024,
                                     kk * 128:(kk + 1) * 128]))
                    for ic in range(NIC):
                        hb = hbuf[ic % 2]
                        for tc in range(2):
                            pg, pu = PS[(g % 2) * 2], PS[(g % 2) * 2 + 1]
                            if g >= 2:
                                S.wait("tensor", ev[g - 2])
                            if g == 0:
                                S.wait("tensor", *(trw + trx), c1, c2)
                            elif half == 1 and ic == 0 and tc == 0:
                                S.wait("tensor", *trx)
                            for kk in range(NKD):
                                mg = S.op("tensor", "matmul", out=pg[:, :],
                                          lhsT=wgf[:, kk * IFF + ic * 128:
                                                   kk * IFF + (ic + 1) * 128],
                                          rhs=xqf[:, kk * 1024 + tc * 512:
                                                  kk * 1024 + (tc + 1) * 512],
                                          start=(kk == 0), stop=(kk == NKD - 1))
                            for kk in range(NKD):
                                mu = S.op("tensor", "matmul", out=pu[:, :],
                                          lhsT=wuf[:, kk * IFF + ic * 128:
                                                   kk * IFF + (ic + 1) * 128],
                                          rhs=xqf[:, kk * 1024 + tc * 512:
                                                  kk * 1024 + (tc + 1) * 512],
                                          start=(kk == 0), stop=(kk == NKD - 1))
                            mm_last = mu
                            S.wait("scalar", mg, sl_prev)
                            sl = S.op("scalar", "activation", out=silu[:, :],
                                      in_=pg[:, :], func=AF.Silu)
                            S.wait("vector", sl, mu, hm0)
                            if tc == 0:
                                S.wait("vector", ev.get(("sp", half * NIC + ic - 2)))
                            hh = S.op("vector", "scalar_tensor_tensor",
                                      out=hb[:, tc * 512:(tc + 1) * 512],
                                      in0=silu[:, :], scalar=1.0, in1=pu[:, :],
                                      op0=Op.mult, op1=Op.mult)
                            sl_prev = hh
                            ev[g] = hh
                            g += 1
                        S.wait("vector", hh)
                        rd = S.op("vector", "tensor_reduce",
                                  out=col(CM_T0 + 11), in_=hb[:, :],
                                  axis=AX.X, op=Op.max,
                                  apply_absolute_value=True)
                        S.wait("vector", rd)
                        hx_last = S.op("vector", "tensor_tensor",
                                       out=col(CM_HMX), in0=col(CM_HMX),
                                       in1=col(CM_T0 + 11), op=Op.max)
                        ev[("hx", half * NIC + ic)] = hx_last
                        S.wait("sync", ev[g - 1])
                        ev[("sp", half * NIC + ic)] = S.dma(
                            "sync", f"hs{ic % 2}",
                            out=h_d[ic * 128:(ic + 1) * 128,
                                    half * 1024:(half + 1) * 1024],
                            in_=hb[:, :])

                hm = emit_armax(cc0 + 1, col(CM_HMX), [hx_last])
                c3, c4 = emit_scale_cols(hm, 1, CM_SH, CM_ISH)

                S.wait("sync", mm_last)
                if first_call:
                    S.wait("sync", *sp_d)
                trd = [S.dma_tr("sync", "trw", out=wgf[:, ic * D:(ic + 1) * D],
                                in_=wdd_bf[:, ic * 128:(ic + 1) * 128])
                       for ic in range(NIC)]
                NSL = 4
                hsl_view = wuf[:, 0:NSL * 2 * IFF].rearrange(
                    "p (s c f) -> p s c f", s=NSL, c=NIC)
                hd_view = h_d.rearrange("(c p) t -> p c t", p=128)
                srec = {}
                a2_prev = None
                ldh = None
                for t in range(NTT):
                    sl4, hh2, sl2 = (t // 2) % NSL, t % 2, t % 2
                    if t == 0:
                        S.wait("sync", ev[("sp", 2 * NIC - 1)],
                               ev[("sp", 2 * NIC - 2)])
                    if t % 2 == 0:
                        if t >= 2 * NSL:
                            S.wait("sync", srec[("a1", t - 2 * NSL + 1)])
                        ldh = S.dma("sync", f"dh{sl4}",
                                    out=hsl_view[:, sl4, :, :],
                                    in_=hd_view[:, :, t * 128:(t + 2) * 128])
                    S.wait("scalar", ldh, a2_prev,
                           *([c3, c4, mm_last] if t == 0 else []))
                    if t >= 2:
                        S.wait("scalar", srec[("mm", t - 2)])
                    a1 = S.op("scalar", "activation",
                              out=A_i8[:, 0:IFF].rearrange(
                                  "p (c f) -> p c f", c=NIC),
                              in_=hsl_view[:, sl4, :, hh2 * 128:(hh2 + 1) * 128],
                              func=AF.Copy, scale=col(CM_ISH))
                    srec[("a1", t)] = a1
                    S.wait("scalar", a1)
                    a2 = S.op("scalar", "activation",
                              out=xqf[:, sl2 * IFF:(sl2 + 1) * IFF],
                              in_=A_i8[:, 0:IFF], func=AF.Copy, scale=col(CM_SH))
                    a2_prev = a2
                    pd = (PS[4], PS[5]) if sl2 == 0 else (PS[6], PS[3])
                    S.wait("tensor", a2)
                    if t == 0:
                        S.wait("tensor", *trd)
                    if t == 1:
                        S.wait("tensor", sl_prev)
                    if t >= 2:
                        S.wait("tensor", srec[("ev", t - 2)])
                    md = None
                    for ic in range(NIC):
                        S.op("tensor", "matmul", out=pd[0][:, :],
                             lhsT=xqf[:, sl2 * IFF + ic * 128:
                                      sl2 * IFF + (ic + 1) * 128],
                             rhs=wgf[:, ic * D: ic * D + 512],
                             start=(ic == 0), stop=(ic == NIC - 1))
                        md = S.op("tensor", "matmul", out=pd[1][:, :],
                                  lhsT=xqf[:, sl2 * IFF + ic * 128:
                                           sl2 * IFF + (ic + 1) * 128],
                                  rhs=wgf[:, ic * D + 512: (ic + 1) * D],
                                  start=(ic == 0), stop=(ic == NIC - 1))
                    srec[("mm", t)] = md
                    S.wait("vector", md)
                    S.op("vector", "tensor_tensor",
                         out=A_x[:, t * D: t * D + 512],
                         in0=A_x[:, t * D: t * D + 512], in1=pd[0][:, :],
                         op=Op.add)
                    e2 = S.op("vector", "tensor_tensor",
                              out=A_x[:, t * D + 512: (t + 1) * D],
                              in0=A_x[:, t * D + 512: (t + 1) * D],
                              in1=pd[1][:, :], op=Op.add)
                    srec[("ev", t)] = e2
                    if out_final:
                        S.wait("sync", e2)
                        srec[("out", t)] = S.dma(
                            "sync", "out", out=out[t * 128:(t + 1) * 128, :],
                            in_=A_x[:, t * D:(t + 1) * D])
                return srec


        def emit_attn():
            nonlocal A_x
            with ExitStack() as aps:
                A_Sb = psb(aps, "A_Sb", [128, NKD * DVH], BF16)
                A_Sin = psb(aps, "A_Sin", [128, NKD * DVH], F32)
                A_Sinb = A_Sb
                A_msk = psb(aps, "A_msk", [128, 2 * CHK], BF16)
                A_zt = psb(aps, "A_zt", [128, CHK], F32)
                A_id = psb(aps, "A_id", [128, 128], BF16)
                tid_f = psb(aps, "tid_f", [128, 128], F32)

                z0 = S.op("vector", "memset", ap=A_zt[:, :], constant=0.0)
                z2_ = S.op("gpsimd", "memset", ap=A_Sb[:, :], constant=0.0)
                dz = S.op("vector", "memset", ap=cols(CM_DAC, 8), constant=1.0)
                mk0 = S.op("gpsimd", "iota", out=A_msk[:, 0:CHK], pattern=[[1, CHK]],
                           base=0, channel_multiplier=-1,
                           allow_small_or_imprecise_dtypes=True)
                mk1 = S.op("gpsimd", "iota", out=A_msk[:, CHK:2 * CHK],
                           pattern=[[1, CHK]], base=-128, channel_multiplier=-1,
                           allow_small_or_imprecise_dtypes=True)
                S.wait("gpsimd", mk0, mk1)
                mk2 = S.op("gpsimd", "tensor_scalar", out=A_msk[:, :],
                           in0=A_msk[:, :], scalar1=0.0, scalar2=1.0 / 16.0,
                           op0=Op.is_ge, op1=Op.mult)
                id0 = S.op("gpsimd", "iota", out=tid_f[:, :], pattern=[[1, 128]],
                           base=0, channel_multiplier=-1,
                           allow_small_or_imprecise_dtypes=True)
                S.wait("gpsimd", id0)
                id1 = S.op("gpsimd", "tensor_scalar", out=A_id[:, :],
                           in0=tid_f[:, :], scalar1=0.0, scalar2=None,
                           op0=Op.is_equal)

                with ExitStack() as p1:
                    tmp2 = psb(p1, "atmp", [128, D], F32)
                    xn1b = psb(p1, "axnb", [128, D], BF16)
                    xn1f = psb(p1, "xn1f", [128, NKD * TOK], BF16)
                    Wa = psb(p1, "Wa", [128, NKD * VD], BF16)
                    stg2 = [psb(p1, f"stg{i}", [128, TOK], BF16) for i in range(2)]
                    zf = psb(p1, "zf", [128, 512], F32)
                    za = psb(p1, "za", [128, 512], F32)
                    zl = psb(p1, "zl", [128, 512], F32)
                    z1b = psb(p1, "z1b", [128, TOK], BF16)
                    gk1a = psb(p1, "gk1a", [128, NKD * 16], BF16)
                    gk2a = psb(p1, "gk2a", [16, KD], BF16)
                    b2c = psb(p1, "b2c", [128, NKD], F32)

                    emit_ln_stats(tmp2, False)
                    sp_xn, sp_x1 = [], []
                    prev_sp = None
                    for t in range(NTT):
                        xt = A_x[:, t * D:(t + 1) * D]
                        S.wait("vector", prev_sp)
                        qn = S.op("vector", "tensor_scalar", out=xn1b[:, :],
                                  in0=xt, scalar1=col(CM_M + t),
                                  scalar2=col(CM_R + t), op0=Op.subtract,
                                  op1=Op.mult)
                        S.wait("sync", qn)
                        s1 = S.dma("sync", "xns", out=xn1_d[t * 128:(t + 1) * 128, :],
                                   in_=xn1b[:, :])
                        prev_sp = s1
                        sp_xn.append(s1)
                        sp_x1.append(S.dma("sync", "x1s", out=x1_d[t * 128:(t + 1) * 128, :],
                                           in_=xt))
                    S.wait("sync", *sp_xn)
                    trx = [S.dma_tr("sync", "trx", out=xn1f[:, kk * TOK:(kk + 1) * TOK],
                                    in_=xn1_d[:, kk * 128:(kk + 1) * 128])
                           for kk in range(NKD)]
                    cmX1.__exit__(None, None, None)
                    lgk1 = [S.dma("gpsimd", "gk", out=gk1a[:, kk * 16:(kk + 1) * 16],
                                  in_=gk1[kk * 128:(kk + 1) * 128, :])
                            for kk in range(NKD)]
                    lgk2 = S.dma("gpsimd", "gk", out=gk2a[:, :], in_=gk2[:, :])
                    lb2 = [S.dma("sync", "gk2", out=b2c[:, kk:kk + 1],
                                 in_=gkb[kk * 128:(kk + 1) * 128][:, None])
                           for kk in range(NKD)]

                    def prepass_fm(wsrc, wcols, dst, act=None, prev=None):
                        if prev:
                            S.wait("gpsimd", prev["mm"])
                        lw = [S.dma("gpsimd", "pw",
                                    out=Wa[:, kk * wcols:(kk + 1) * wcols],
                                    in_=wsrc[kk * 128:(kk + 1) * 128, :])
                              for kk in range(NKD)]
                        nf = wcols // 128
                        sp = []
                        pe_prev = {}
                        mm = None
                        for fc in range(nf):
                            sg = stg2[fc % 2]
                            for tc4 in range(4):
                                pp = PS[(fc * 4 + tc4) % 2]
                                if (fc, tc4) == (0, 0):
                                    S.wait("tensor", *(lw + trx),
                                           *( [prev["ev"]] if prev else []))
                                k = (fc * 4 + tc4) % 2
                                if fc * 4 + tc4 >= 2:
                                    S.wait("tensor", pe_prev[k])
                                for kk in range(NKD):
                                    mm = S.op("tensor", "matmul", out=pp[:, :],
                                              lhsT=Wa[:, kk * wcols + fc * 128:
                                                      kk * wcols + (fc + 1) * 128],
                                              rhs=xn1f[:, kk * TOK + tc4 * 512:
                                                       kk * TOK + (tc4 + 1) * 512],
                                              start=(kk == 0), stop=(kk == NKD - 1))
                                S.wait("scalar", mm)
                                if fc <= 1 and tc4 == 0 and prev:
                                    S.wait("scalar", prev["sp2"] if fc == 0 else prev["sp"])
                                if fc > 1 and tc4 == 0:
                                    S.wait("scalar", sp[-2])
                                ev = S.op("scalar", "activation",
                                          out=sg[:, tc4 * 512:(tc4 + 1) * 512],
                                          in_=pp[:, :],
                                          func=(act or AF.Copy))
                                pe_prev[k] = ev
                            S.wait("sync", pe_prev[(fc * 4 + 3) % 2])
                            sp.append(S.dma("sync", f"pps{fc % 2}",
                                            out=dst[fc * 128:(fc + 1) * 128, :],
                                            in_=sg[:, :]))
                        return dict(sp=sp[-1], sp2=sp[-2], sps=sp, mm=mm, ev=ev)

                    def prepass_tm(wsrc, dst, act=None, prev=None):
                        if prev:
                            S.wait("gpsimd", prev["mm"])
                        lw = [S.dma("gpsimd", "pw", out=Wa[:, kk * VD:(kk + 1) * VD],
                                    in_=wsrc[kk * 128:(kk + 1) * 128, :])
                              for kk in range(NKD)]
                        sp = []
                        pe_prev = {}
                        mm = None
                        for t in range(NTT):
                            sg = stg2[t % 2]
                            for vc in range(4):
                                pp = PS[(t * 4 + vc) % 2]
                                if (t, vc) == (0, 0):
                                    S.wait("tensor", *(lw + trx),
                                           *( [prev["ev"]] if prev else []))
                                k = (t * 4 + vc) % 2
                                if t * 4 + vc >= 2:
                                    S.wait("tensor", pe_prev[k])
                                for kk in range(NKD):
                                    mm = S.op("tensor", "matmul", out=pp[:, :],
                                              lhsT=xn1f[:, kk * TOK + t * 128:
                                                        kk * TOK + (t + 1) * 128],
                                              rhs=Wa[:, kk * VD + vc * 512:
                                                     kk * VD + (vc + 1) * 512],
                                              start=(kk == 0), stop=(kk == NKD - 1))
                                S.wait("scalar", mm)
                                if t <= 1 and vc == 0 and prev:
                                    S.wait("scalar", prev["sp2"] if t == 0 else prev["sp"])
                                if t > 1 and vc == 0:
                                    S.wait("scalar", sp[-2])
                                ev = S.op("scalar", "activation",
                                          out=sg[:, vc * 512:(vc + 1) * 512],
                                          in_=pp[:, :], func=(act or AF.Copy))
                                pe_prev[k] = ev
                            S.wait("sync", pe_prev[(t * 4 + 3) % 2])
                            sp.append(S.dma("sync", f"pps{t % 2}",
                                            out=dst[t * 128:(t + 1) * 128, :],
                                            in_=sg[:, :]))
                        return dict(sp=sp[-1], sp2=sp[-2], sps=sp, mm=mm, ev=ev)

                    pq = prepass_fm(Wq, 1024, q_d)
                    pk = prepass_fm(Wk, 1024, k_d, prev=pq)
                    pv = prepass_tm(Wv, v_d, prev=pk)
                    psg = prepass_tm(Wg, sg_d, act=AF.Silu, prev=pv)

                    z1evs = []
                    for tc4 in range(4):
                        pp = PS[2]
                        if tc4 > 0:
                            S.wait("tensor", z1evs[-1])
                        if tc4 == 0:
                            S.wait("tensor", *lgk1, lgk2)
                        mm = None
                        for kk in range(NKD):
                            mm = S.op("tensor", "matmul", out=pp[0:16, :],
                                      lhsT=gk1a[:, kk * 16:(kk + 1) * 16],
                                      rhs=xn1f[:, kk * TOK + tc4 * 512:
                                               kk * TOK + (tc4 + 1) * 512],
                                      start=(kk == 0), stop=(kk == NKD - 1))
                        S.wait("scalar", mm)
                        ev = S.op("scalar", "activation",
                                  out=z1b[0:16, tc4 * 512:(tc4 + 1) * 512],
                                  in_=pp[0:16, :], func=AF.Copy)
                        z1evs.append(ev)
                    gsp = []
                    gprev = None
                    gprev_pair = {}
                    for fc in range(NKD):
                        for tc4 in range(4):
                            pp = PS[3 + ((fc * 4 + tc4) % 2)]
                            if fc * 4 + tc4 >= 2:
                                S.wait("tensor", gprev_pair[(fc * 4 + tc4) % 2])
                            if (fc, tc4) == (0, 0):
                                S.wait("tensor", z1evs[-1], lgk2)
                            if tc4 == 0 and fc <= 1:
                                S.wait("vector", psg["sp2"] if fc == 0 else psg["sp"])
                            mz = S.op("tensor", "matmul", out=pp[:, :],
                                      lhsT=gk2a[:, fc * 128:(fc + 1) * 128],
                                      rhs=z1b[0:16, tc4 * 512:(tc4 + 1) * 512],
                                      start=True, stop=True)
                            S.wait("vector", mz, *( [lb2[-1]] if tc4 == 0 else []))
                            if gprev is not None:
                                S.wait("vector", gprev)
                            zb = S.op("vector", "tensor_scalar", out=zf[:, :],
                                      in0=pp[:, :], scalar1=b2c[:, fc:fc + 1],
                                      scalar2=None, op0=Op.add)
                            S.wait("scalar", zb)
                            av = S.op("scalar", "activation", out=za[:, :],
                                      in_=zf[:, :], func=AF.Abs)
                            S.wait("scalar", av)
                            ee = S.op("scalar", "activation", out=za[:, :],
                                      in_=za[:, :], func=AF.Exp, scale=-1.0)
                            S.wait("scalar", ee)
                            le = S.op("scalar", "activation", out=zl[:, :],
                                      in_=za[:, :], func=AF.Ln, bias=1.0)
                            S.wait("vector", le)
                            mn = S.op("vector", "tensor_scalar", out=zf[:, :],
                                      in0=zf[:, :], scalar1=0.0, scalar2=None,
                                      op0=Op.min)
                            S.wait("vector", mn)
                            gg = S.op("vector", "tensor_tensor",
                                      out=stg2[fc % 2][:, tc4 * 512:(tc4 + 1) * 512],
                                      in0=zf[:, :], in1=zl[:, :], op=Op.subtract)
                            gprev = gg
                            gprev_pair[(fc * 4 + tc4) % 2] = gg
                        S.wait("sync", gprev)
                        gsp.append(S.dma("sync", f"pps{fc % 2}",
                                         out=g_d[fc * 128:(fc + 1) * 128, :],
                                         in_=stg2[fc % 2][:, :]))
                        if fc >= 1:
                            S.wait("vector", gsp[-2])
                    S.wait("sync", *(psg["sps"] + gsp))
                S.barrier()

                with ExitStack() as p2:
                    A_S = psb(p2, "A_S", [128, NKD * DVH], F32)
                    zS = S.op("vector", "memset", ap=A_S[:, :], constant=0.0)
                    db = []
                    for i in range(2):
                        db.append(dict(
                            b=psb(p2, f"b{i}", [128, 8 * CHK], F32),
                            g=psb(p2, f"g{i}", [128, 8 * CHK], BF16),
                            q=psb(p2, f"q{i}", [128, 8 * CHK], BF16),
                            k=psb(p2, f"k{i}", [128, 8 * CHK], BF16),
                            eb=psb(p2, f"eb{i}", [128, 8 * CHK], BF16),
                            emb=psb(p2, f"em{i}", [128, 8 * CHK], BF16),
                            qr=psb(p2, f"qr{i}", [128, 8 * CHK], BF16),
                            kr=psb(p2, f"kr{i}", [128, 8 * CHK], BF16),
                            qf=psb(p2, f"qf{i}", [128, 8 * CHK], BF16),
                            v=[psb(p2, f"v{i}_{j}", [128, VD], BF16)
                               for j in range(2)],
                        ))
                    A_kh = psb(p2, "A_kh", [128, 8 * CHK], BF16)
                    A_qg = psb(p2, "A_qg", [128, 8 * CHK], BF16)
                    A_ktr = psb(p2, "A_ktr", [128, 2 * KD], BF16)
                    A_pm = psb(p2, "A_pm", [128, 2 * CHK], BF16)
                    A_ol = [psb(p2, f"A_ol{i}", [128, VD], BF16) for i in range(2)]

                    gv = g_d.rearrange("(c p) t -> p c t", p=128)
                    qv = q_d.rearrange("(c p) t -> p c t", p=128)
                    kv = k_d.rearrange("(c p) t -> p c t", p=128)
                    qgv = qg_d.rearrange("(c p) t -> p c t", p=128)
                    khv = kh_d.rearrange("(c p) t -> p c t", p=128)
                    qgsv = A_qg.rearrange("p (c t) -> p c t", t=CHK)
                    khsv = A_kh.rearrange("p (c t) -> p c t", t=CHK)

                    prevc = {}
                    for c in range(NCH):
                        if PARANOID:
                            S.barrier()
                        cp = c % 2
                        B = db[cp]
                        base = c * CHK
                        if c > 1:
                            S.wait("sync", prevc[("scan", cp)],
                                   prevc[("qglast", cp)], prevc[("vlast", cp)])
                        ldg = S.dma("sync", f"cg{cp}",
                                    out=B["g"].rearrange("p (c t) -> p c t", t=CHK),
                                    in_=gv[:, :, base:base + CHK])
                        ldq = S.dma("sync", f"cq{cp}",
                                    out=B["q"].rearrange("p (c t) -> p c t", t=CHK),
                                    in_=qv[:, :, base:base + CHK])
                        ldk = S.dma("sync", f"ck{cp}",
                                    out=B["k"].rearrange("p (c t) -> p c t", t=CHK),
                                    in_=kv[:, :, base:base + CHK])
                        ldv = [S.dma("sync", f"cv{cp}{i}", out=B["v"][i][:, :],
                                     in_=v_d[base + i * 128:base + (i + 1) * 128, :])
                               for i in range(2)]
                        ebx = prevc.get(("ebx", cp))
                        S.wait("vector", ldg, *(list(ebx) if ebx else []))
                        sc = None
                        for fc in range(NKD):
                            sc = S.op("vector", "tensor_tensor_scan",
                                      out=B["b"][:, fc * CHK:(fc + 1) * CHK],
                                      data0=B["g"].rearrange(
                                          "p (c t) -> p c t", t=CHK)[:, fc, :],
                                      data1=A_zt[:, :], initial=0.0,
                                      op0=Op.add, op1=Op.add)
                        prevc[("scan", cp)] = sc
                        S.wait("scalar", sc, *(prevc.get(("qk", cp)) or []))
                        ebr = S.op("scalar", "activation", out=B["eb"][:, :],
                                   in_=B["b"][:, :], func=AF.Exp, scale=1.0 / 16)
                        embr = S.op("scalar", "activation", out=B["emb"][:, :],
                                    in_=B["b"][:, :], func=AF.Exp, scale=-1.0 / 16)
                        if c > 0:
                            S.wait("scalar", prevc[("stup", 7)])
                        ebcr = S.op("scalar", "activation", out=cols(CM_EBC, 8),
                                    in_=B["b"].rearrange(
                                        "p (f t) -> p f t", t=CHK)[:, :, CHK - 1],
                                    func=AF.Exp, scale=1.0 / 16)
                        prevc[("ebx", cp)] = (ebr, embr, ebcr)
                        S.wait("vector", ldq, ldk, ebr, embr,
                               *( [prevc[("m2last", cp)], prevc[("molast", cp)]]
                                  if c > 1 else []))
                        qrr = S.op("vector", "tensor_tensor", out=B["qr"][:, :],
                                   in0=B["q"][:, :], in1=B["eb"][:, :], op=Op.mult)
                        krr = S.op("vector", "tensor_tensor", out=B["kr"][:, :],
                                   in0=B["k"][:, :], in1=B["emb"][:, :], op=Op.mult)
                        S.wait("vector", qrr)
                        qfr = S.op("vector", "tensor_scalar", out=B["qf"][:, :],
                                   in0=B["qr"][:, :], scalar1=1.0 / 16,
                                   scalar2=None, op0=Op.mult)
                        prevc[("qk", cp)] = (qrr, krr, qfr)
                        S.wait("vector", krr, ebcr,
                               *( [prevc["spkh"], prevc["spqg"]] if c > 0 else []))
                        khl = None
                        for fc in range(NKD):
                            khl = S.op("vector", "tensor_scalar",
                                       out=A_kh[:, fc * CHK:(fc + 1) * CHK],
                                       in0=B["kr"][:, fc * CHK:(fc + 1) * CHK],
                                       scalar1=col(CM_EBC + fc), scalar2=None,
                                       op0=Op.mult)
                        S.wait("vector", qfr)
                        qgl = None
                        for fc in range(NKD):
                            qgl = S.op("vector", "tensor_scalar",
                                       out=A_qg[:, fc * CHK:(fc + 1) * CHK],
                                       in0=B["qf"][:, fc * CHK:(fc + 1) * CHK],
                                       scalar1=col(CM_DAC + fc), scalar2=None,
                                       op0=Op.mult)
                        prevc[("qglast", cp)] = qgl
                        S.wait("vector", qgl)
                        dacu = S.op("vector", "tensor_tensor",
                                    out=cols(CM_DAC, 8), in0=cols(CM_DAC, 8),
                                    in1=cols(CM_EBC, 8), op=Op.mult)
                        prevc["dacu"] = dacu
                        prevc["dacu_last"] = dacu
                        S.wait("sync", qgl, khl)
                        spqg = S.dma("sync", "sqg", out=qgv[:, :, base:base + CHK],
                                     in_=qgsv[:, :, :])
                        if c > 0:
                            S.wait("sync", prevc["ktr"][1], prevc[("vlast", 1 - cp)])
                        spkh = S.dma("sync", "skh", out=khv[:, :, :],
                                     in_=khsv[:, :, :])
                        prevc["spqg"] = spqg
                        prevc["spkh"] = spkh
                        S.wait("sync", spkh)
                        ktr = [S.dma_tr("sync", "ktr",
                                        out=A_ktr[:, i * KD:(i + 1) * KD],
                                        in_=kh_d[:, i * 128:(i + 1) * 128])
                               for i in range(2)]
                        prevc["ktr"] = ktr

                        o_evs = []
                        for h in range(NH):
                            pmevs = []
                            for sb2 in range(2):
                                if (c, h, sb2) != (0, 0, 0):
                                    S.wait("tensor", prevc.get("pmev"))
                                else:
                                    S.wait("tensor", mk2, zS, z2_)
                                    S.wait("vector", mk2)
                                if h == 0 and sb2 == 0:
                                    S.wait("tensor", qgl, ldv[0], ldv[1])
                                if sb2 == 0 and prevc.get("molast") is not None:
                                    S.wait("vector", prevc["molast"])
                                m1 = S.op("tensor", "matmul",
                                          out=PS[2][:, 0:CHK],
                                          lhsT=B["kr"][:, (2 * h) * CHK + sb2 * 128:
                                                       (2 * h) * CHK + sb2 * 128 + 128],
                                          rhs=B["qr"][:, (2 * h) * CHK:(2 * h + 1) * CHK],
                                          start=True, stop=False)
                                m2 = S.op("tensor", "matmul",
                                          out=PS[2][:, 0:CHK],
                                          lhsT=B["kr"][:, (2 * h + 1) * CHK + sb2 * 128:
                                                       (2 * h + 1) * CHK + sb2 * 128 + 128],
                                          rhs=B["qr"][:, (2 * h + 1) * CHK:(2 * h + 2) * CHK],
                                          start=False, stop=True)
                                prevc[("m2last", cp)] = m2
                                prevc["m2last"] = m2
                                S.wait("vector", m2)
                                pmev = S.op("vector", "tensor_tensor",
                                            out=A_pm[:, sb2 * CHK:(sb2 + 1) * CHK],
                                            in0=PS[2][:, 0:CHK],
                                            in1=A_msk[:, sb2 * CHK:(sb2 + 1) * CHK],
                                            op=Op.mult)
                                prevc["pmev"] = pmev
                                pmevs.append(pmev)
                            for th in range(2):
                                po = PS[3 + th]
                                S.wait("tensor", pmevs[0], pmevs[1],
                                       prevc.get(("olastuse", th)),
                                       *( [prevc.get(("sbcp", 2 * h)),
                                           prevc.get(("sbcp", 2 * h + 1))]
                                          if c > 0 else []))
                                S.op("tensor", "matmul", out=po[:, :],
                                     lhsT=A_pm[:, 0 * CHK + th * 128:
                                               0 * CHK + th * 128 + 128],
                                     rhs=B["v"][0][:, h * DVH:(h + 1) * DVH],
                                     start=True, stop=False)
                                S.op("tensor", "matmul", out=po[:, :],
                                     lhsT=A_pm[:, 1 * CHK + th * 128:
                                               1 * CHK + th * 128 + 128],
                                     rhs=B["v"][1][:, h * DVH:(h + 1) * DVH],
                                     start=False, stop=False)
                                S.op("tensor", "matmul", out=po[:, :],
                                     lhsT=B["qf"][:, (2 * h) * CHK + th * 128:
                                                  (2 * h) * CHK + th * 128 + 128],
                                     rhs=A_Sb[:, (2 * h) * DVH:(2 * h + 1) * DVH],
                                     start=False, stop=False)
                                mo = S.op("tensor", "matmul", out=po[:, :],
                                          lhsT=B["qf"][:, (2 * h + 1) * CHK + th * 128:
                                                       (2 * h + 1) * CHK + th * 128 + 128],
                                          rhs=A_Sb[:, (2 * h + 1) * DVH:(2 * h + 2) * DVH],
                                          start=False, stop=True)
                                S.wait("scalar", mo)
                                if c > 0 or h > 0:
                                    S.wait("scalar", prevc.get(("osp", th)))
                                oev = S.op("scalar", "activation",
                                           out=A_ol[th][:, h * DVH:(h + 1) * DVH],
                                           in_=po[:, :], func=AF.Copy)
                                prevc[("oev", th, h)] = oev
                                prevc[("olastuse", th)] = oev
                                prevc[("olast", th)] = oev
                                o_evs.append(oev)
                                prevc[("mo", h, th)] = mo
                            for fsub in range(2):
                                fc = 2 * h + fsub
                                pu = PS[5 + fsub]
                                S.wait("tensor", *ktr,
                                       prevc.get(("stup", (fc - 2) % 8)))
                                S.op("tensor", "matmul", out=pu[:, :],
                                     lhsT=A_ktr[:, 0 * KD + fc * 128:
                                               0 * KD + (fc + 1) * 128],
                                     rhs=B["v"][0][:, h * DVH:(h + 1) * DVH],
                                     start=True, stop=False)
                                mu = S.op("tensor", "matmul", out=pu[:, :],
                                          lhsT=A_ktr[:, 1 * KD + fc * 128:
                                                    1 * KD + (fc + 1) * 128],
                                          rhs=B["v"][1][:, h * DVH:(h + 1) * DVH],
                                          start=False, stop=True)
                                prevc[("vlast", cp)] = mu
                                S.wait("vector", mu, ebcr,
                                       prevc.get(("sbcp", fc)))
                                stup = S.op("vector", "scalar_tensor_tensor",
                                            out=A_S[:, fc * DVH:(fc + 1) * DVH],
                                            in0=A_S[:, fc * DVH:(fc + 1) * DVH],
                                            scalar=col(CM_EBC + fc),
                                            in1=pu[:, :], op0=Op.mult, op1=Op.add)
                                prevc[("stup", fc)] = stup
                                S.wait("gpsimd", stup, prevc[("mo", h, 0)],
                                       prevc[("mo", h, 1)])
                                sbcp = S.op("gpsimd", "tensor_copy",
                                            out=A_Sb[:, fc * DVH:(fc + 1) * DVH],
                                            in_=A_S[:, fc * DVH:(fc + 1) * DVH])
                                prevc[("sbcp", fc)] = sbcp
                            prevc["molast"] = prevc[("mo", h, 1)]
                            prevc[("molast", cp)] = prevc[("mo", h, 1)]
                        for th in range(2):
                            S.wait("sync", prevc[("olast", th)], *o_evs)
                            osp = S.dma("sync", f"ol{th}",
                                        out=ol_d[base + th * 128:base + th * 128 + 128, :],
                                        in_=A_ol[th][:, :])
                            prevc[("osp", th)] = osp

                    S.wait("sync", prevc[("stup", 6)], prevc[("stup", 7)],
                           prevc["dacu_last"])
                    S.wait("sync", wag)
                    ccv = cc_sin.ap()[0:KD, :].rearrange("(f p) w -> p f w", p=128)
                    sps = [S.dma("sync", "ccs", out=ccv[:, fc, :],
                                 in_=A_S[:, fc * DVH:(fc + 1) * DVH])
                           for fc in range(NKD)]
                    S.wait("vector", prevc[("ebx", 1)][1])
                    zpad = S.op("vector", "memset", ap=db[1]["b"][:, 0:504],
                                constant=0.0)
                    S.wait("sync", zpad)
                    sps.append(S.dma("sync", "ccs", out=cc_sin.ap()[KD:KD + 128, 0:8],
                                     in_=cols(CM_DAC, 8)))
                    spd = S.dma("sync", "ccs", out=cc_sin.ap()[KD:KD + 128, 8:512],
                                in_=db[1]["b"][:, 0:504])
                    S.wait("gpsimd", *sps, spd)
                    ag = S.cc(lambda g: g.collective_compute(
                        "AllGather", Op.bypass,
                        replica_groups=[[0, 1, 2, 3], [4, 5, 6, 7]],
                        ins=[cc_sin.ap().opt()], outs=[cc_sout.ap().opt()]))
                    on2 = S.op("vector", "memset", ap=A_row[0:1, 40:168],
                               constant=1.0)
                    S.wait("tensor", on2, ld_gm, prevc[("stup", 6)],
                           prevc[("stup", 7)])
                    gmms = [S.op("tensor", "matmul", out=PS[6][0:128, j:j + 1],
                                 lhsT=A_row[0:1, 40:168],
                                 rhs=A_row[0:1, 8 + j:9 + j], start=True,
                                 stop=True) for j in range(4)]
                    S.wait("vector", *gmms)
                    gmc = [S.op("vector", "tensor_copy", out=col(CM_GM + j),
                                in_=PS[6][:, j:j + 1]) for j in range(4)]
                    zin = S.op("vector", "memset", ap=A_Sin[:, :], constant=0.0)
                    zdc = S.op("vector", "memset", ap=cols(CM_T0, 8), constant=1.0)
                    with ExitStack() as p2b:
                        ust = [psb(p2b, f"ust{i}", [128, 512], F32) for i in range(4)]
                        djc = psb(p2b, "djc", [128, 32], F32)
                        S.wait("sync", ag)
                        ROWS = KD + 128
                        ldjs = [S.dma("sync", "cmpd", out=djc[:, j * 8:(j + 1) * 8],
                                      in_=cc_sout.ap()[j * ROWS + KD:
                                                       j * ROWS + KD + 128, 0:8])
                                for j in range(4)]
                        accs = {}
                        for jj in range(4):
                            j = 3 - jj
                            for fc in range(NKD):
                                sl = (jj * NKD + fc) % 4
                                if jj * NKD + fc >= 4:
                                    S.wait("sync", accs[jj * NKD + fc - 4],
                                           accs[("dcu", jj * NKD + fc - 4)])
                                lu = S.dma("sync", f"cmp{sl}", out=ust[sl][:, :],
                                           in_=cc_sout.ap()[j * ROWS + fc * 128:
                                                            j * ROWS + (fc + 1) * 128, 0:512])
                                S.wait("vector", lu, ldjs[-1], zin, zdc,
                                       gmc[j] if jj == 0 and fc == 0 else None)
                                dg = S.op("vector", "tensor_tensor",
                                          out=col(CM_T0 + 8), in0=col(CM_T0 + fc),
                                          in1=col(CM_GM + j), op=Op.mult)
                                S.wait("vector", dg)
                                acc = S.op("vector", "scalar_tensor_tensor",
                                           out=A_Sin[:, fc * DVH:(fc + 1) * DVH],
                                           in0=ust[sl][:, :], scalar=col(CM_T0 + 8),
                                           in1=A_Sin[:, fc * DVH:(fc + 1) * DVH],
                                           op0=Op.mult, op1=Op.add)
                                accs[jj * NKD + fc] = acc
                                t1 = S.op("vector", "tensor_tensor",
                                          out=col(CM_T0 + 9),
                                          in0=djc[:, j * 8 + fc:j * 8 + fc + 1],
                                          in1=col(CM_GM + j), op=Op.mult)
                                t2 = S.op("vector", "tensor_scalar",
                                          out=col(CM_T0 + 10), in0=col(CM_GM + j),
                                          scalar1=-1.0, scalar2=1.0, op0=Op.mult,
                                          op1=Op.add)
                                S.wait("vector", t1, t2)
                                t3 = S.op("vector", "tensor_tensor",
                                          out=col(CM_T0 + 9), in0=col(CM_T0 + 9),
                                          in1=col(CM_T0 + 10), op=Op.add)
                                S.wait("vector", t3)
                                dcu = S.op("vector", "tensor_tensor",
                                           out=col(CM_T0 + fc), in0=col(CM_T0 + fc),
                                           in1=col(CM_T0 + 9), op=Op.mult)
                                accs[("dcu", jj * NKD + fc)] = dcu
                        sinb = []
                        for fc in range(NKD):
                            S.wait("vector", accs[3 * NKD + fc],
                                   prevc[("sbcp", fc)], prevc["molast"])
                            sinb.append(S.op("vector", "tensor_copy",
                                             out=A_Sinb[:, fc * DVH:(fc + 1) * DVH],
                                             in_=A_Sin[:, fc * DVH:(fc + 1) * DVH]))
                S.barrier()

                A_x = stk.enter_context(
                    nc.sbuf_tensor("A_x2", [128, NTT * D], F32, side="right"))
                with ExitStack() as p3:
                    Wof = psb(p3, "Wof", [128, 16 * D], BF16)
                    gnwc = psb(p3, "gnwc", [128, 4], F32)
                    p3a = p3.enter_context(ExitStack())
                    sgb = [psb(p3a, f"A_sg{i}", [128, VD], BF16) for i in range(2)]
                    olb = [psb(p3a, f"A_olr{i}", [128, VD], BF16) for i in range(2)]
                    qgb = [psb(p3a, f"A_qgs{i}", [128, 2 * KD], BF16) for i in range(2)]
                    ofb = [psb(p3a, f"A_of{i}", [128, VD], F32) for i in range(2)]
                    A_sq = psb(p3a, "A_sq", [128, 512], F32)
                    ogb = [psb(p3a, f"A_og{i}", [128, VD], BF16) for i in range(2)]
                    ogt = [psb(p3a, f"A_ogt{i}", [128, VD], BF16) for i in range(2)]

                    lwo = [S.dma("gpsimd", "gw", out=Wof[:, fc * D:(fc + 1) * D],
                                 in_=Wo[fc * 128:(fc + 1) * 128, :])
                           for fc in range(16)]
                    lgn = [S.dma("sync", "gk2", out=gnwc[:, a:a + 1],
                                 in_=gnw[a * 128:(a + 1) * 128][:, None])
                           for a in range(4)]
                    wosc = []
                    for fc in range(16):
                        S.wait("vector", lwo[-1], lgn[-1])
                        wosc.append(S.op("vector", "tensor_scalar",
                                         out=Wof[:, fc * D:(fc + 1) * D],
                                         in0=Wof[:, fc * D:(fc + 1) * D],
                                         scalar1=gnwc[:, fc % 4:fc % 4 + 1],
                                         scalar2=None, op0=Op.mult))

                    qgv2 = qg_d.rearrange("(f p) t -> p f t", p=128)
                    evs = {}
                    prev4 = {}
                    for t in range(NTT):
                        if PARANOID:
                            S.barrier()
                        par = t % 2
                        qpar, qh = (t // 2) % 2, t % 2
                        A_sg, A_olr, A_og = sgb[par], olb[par], ogb[par]
                        A_of = ofb[par]
                        A_qgs = qgb[qpar]
                        if t >= 2:
                            S.wait("sync", prev4[("ogsp", t - 2)],
                                   prev4[("ofl", t - 2)], prev4[("mc", t - 2)])
                        lsg = S.dma("sync", f"c4a{par}", out=A_sg[:, :],
                                    in_=sg_d[t * 128:(t + 1) * 128, :])
                        lol = S.dma("sync", f"c4b{par}", out=A_olr[:, :],
                                    in_=ol_d[t * 128:(t + 1) * 128, :])
                        if t % 2 == 0:
                            if t >= 4:
                                S.wait("sync", prev4[("mc", t - 3)])
                            lqg = S.dma("sync", f"c4c{qpar}",
                                        out=A_qgs.rearrange("p (f t) -> p f t", t=256),
                                        in_=qgv2[:, :, t * 128:(t + 2) * 128])
                            prev4["lqg"] = lqg
                        lqg = prev4["lqg"]
                        lx1 = S.dma("sync", f"c4x{par}",
                                    out=A_x[:, t * D:(t + 1) * D],
                                    in_=x1_d[t * 128:(t + 1) * 128, :])
                        prev4["lx1", t] = lx1
                        prev4["lx1last"] = lx1
                        for h in range(NH):
                            pc_ = PS[h % 2]
                            S.wait("tensor", lqg,
                                   *( [sinb[2 * h], sinb[2 * h + 1]] if t == 0 else []),
                                   prev4.get(("ofp", h % 2)))
                            S.op("tensor", "matmul", out=pc_[:, :],
                                 lhsT=A_qgs[:, (2 * h) * 256 + qh * 128:
                                            (2 * h) * 256 + qh * 128 + 128],
                                 rhs=A_Sinb[:, (2 * h) * DVH:(2 * h + 1) * DVH],
                                 start=True, stop=False)
                            mc = S.op("tensor", "matmul", out=pc_[:, :],
                                      lhsT=A_qgs[:, (2 * h + 1) * 256 + qh * 128:
                                                 (2 * h + 1) * 256 + qh * 128 + 128],
                                      rhs=A_Sinb[:, (2 * h + 1) * DVH:(2 * h + 2) * DVH],
                                      start=False, stop=True)
                            S.wait("vector", mc, lol,
                                   prev4.get(("sq", h, par)),
                                   prev4.get(("og", h, par)))
                            ofadd = S.op("vector", "tensor_tensor",
                                         out=A_of[:, h * DVH:(h + 1) * DVH],
                                         in0=PS[h % 2][:, :],
                                         in1=A_olr[:, h * DVH:(h + 1) * DVH],
                                         op=Op.add)
                            prev4[("ofadd", h)] = ofadd
                            prev4[("ofp", h % 2)] = ofadd
                            S.wait("scalar", ofadd,
                                   prev4.get(("sq", h - 1, par)))
                            sqr = S.op("scalar", "activation", out=A_sq[:, :],
                                       in_=A_of[:, h * DVH:(h + 1) * DVH],
                                       func=AF.Square,
                                       accum_out=col(CM_T0 + 12 + par * 4 + h))
                            prev4[("sq", h, par)] = sqr
                        S.wait("scalar", prev4[("sq", 3, par)])
                        rms = S.op("scalar", "activation",
                                   out=cols(CM_RMS + par * 4, 4),
                                   in_=cols(CM_T0 + 12 + par * 4, 4), func=AF.Sqrt,
                                   scale=1.0 / DVH, bias=EPS)
                        S.wait("vector", rms)
                        rmr = S.op("vector", "reciprocal",
                                   out=cols(CM_RMS + par * 4, 4),
                                   in_=cols(CM_RMS + par * 4, 4))
                        oglast = None
                        for h in range(NH):
                            S.wait("vector", rmr, lsg,
                                   *( [prev4[("ogsp", t - 2)]] if (h == 0 and t >= 2) else []))
                            og = S.op("vector", "scalar_tensor_tensor",
                                      out=A_og[:, h * DVH:(h + 1) * DVH],
                                      in0=A_of[:, h * DVH:(h + 1) * DVH],
                                      scalar=col(CM_RMS + par * 4 + h),
                                      in1=A_sg[:, h * DVH:(h + 1) * DVH],
                                      op0=Op.mult, op1=Op.mult)
                            prev4[("og", h, par)] = og
                            oglast = og
                        A_ogt = ogt[par]
                        evlast = None
                        for b2 in range(2):
                            trs = None
                            for i in range(8):
                                fc16 = b2 * 8 + i
                                if i == 0:
                                    S.wait("tensor", oglast, id1,
                                           prev4.get(("pbt", 1 - b2 if b2 else 1)),
                                           *( [prev4.get(("womd", t - 2))]
                                              if (b2 == 0 and t >= 2) else []))
                                trs = S.op("tensor", "transpose",
                                           out=PBT[:, i * 128:(i + 1) * 128],
                                           in_=A_og[:, fc16 * 128:(fc16 + 1) * 128],
                                           identity=A_id[:, :])
                            S.wait("vector", trs)
                            pbe = S.op("vector", "tensor_copy",
                                       out=A_ogt[:, b2 * 1024:(b2 + 1) * 1024],
                                       in_=PBT[:, :])
                            prev4[("pbt", b2)] = pbe
                        pw = (PS[2], PS[3]) if par == 0 else (PS[4], PS[5])
                        S.wait("tensor", prev4[("pbt", 0)], prev4[("pbt", 1)],
                               lwo[-1] if t == 0 else None,
                               *( [prev4.get(("woev", t - 2))] if t >= 2 else []))
                        if t == 0:
                            S.wait("tensor", *wosc)
                        md = None
                        for fc16 in range(16):
                            S.op("tensor", "matmul", out=pw[0][:, :],
                                 lhsT=A_ogt[:, fc16 * 128:(fc16 + 1) * 128],
                                 rhs=Wof[:, fc16 * D:fc16 * D + 512],
                                 start=(fc16 == 0), stop=(fc16 == 15))
                            md = S.op("tensor", "matmul", out=pw[1][:, :],
                                      lhsT=A_ogt[:, fc16 * 128:(fc16 + 1) * 128],
                                      rhs=Wof[:, fc16 * D + 512:(fc16 + 1) * D],
                                      start=(fc16 == 0), stop=(fc16 == 15))
                        prev4[("womd", t)] = md
                        S.wait("vector", md, prev4[("lx1", t)])
                        S.op("vector", "tensor_tensor",
                             out=A_x[:, t * D:t * D + 512],
                             in0=A_x[:, t * D:t * D + 512],
                             in1=pw[0][:, :], op=Op.add)
                        woev = S.op("vector", "tensor_tensor",
                                    out=A_x[:, t * D + 512:(t + 1) * D],
                                    in0=A_x[:, t * D + 512:(t + 1) * D],
                                    in1=pw[1][:, :], op=Op.add)
                        prev4[("woev", t)] = woev
                        evs[(1, t)] = woev
                        prev4[("ogsp", t)] = woev
                        prev4[("ofl", t)] = prev4[("ofadd", 3)]
                        prev4[("mc", t)] = mc

                    p3a.close()
            S.barrier()
            S.barrier()
            return evs

        S.wait("vector", *ld_x)
        S.wait("scalar", *ld_x)
        ffn1 = emit_ffn(True, 0, out_final=(STAGE == 1))
        if STAGE == 1:
            lasts = [ffn1[("out", t)] for t in range(NTT)]
            if DEBUG:
                S.wait("gpsimd", *lasts)
                lasts = [
                    S.dma("gpsimd", "dbg", out=d_xq, in_=xq_d),
                    S.dma("gpsimd", "dbg", out=d_h, in_=h_d[0:256, :]),
                    S.dma("gpsimd", "dbg", out=d_wg, in_=wdg_bf[0:256, :]),
                    S.dma("gpsimd", "dbg", out=d_cols, in_=A_cols[:, :]),
                    S.dma("gpsimd", "dbg", out=d_row, in_=A_row[:, :]),
                ]
                S.wait("gpsimd", *lasts)
            S.wait("sync", *lasts)
            S.finalize(nc, block, stk)
            return nc

        S.barrier()
        attn_evs = emit_attn()
        if STAGE == 2:
            finals = []
            for t in range(NTT):
                S.wait("sync", attn_evs[(1, t)])
                finals.append(S.dma("sync", "out", out=out[t * 128:(t + 1) * 128, :],
                                    in_=A_x[:, t * D:(t + 1) * D]))
            if DEBUG2:
                S.wait("gpsimd", *finals)
                finals = [
                    S.dma("gpsimd", "dbg", out=d_q, in_=q_d),
                    S.dma("gpsimd", "dbg", out=d_k, in_=k_d),
                    S.dma("gpsimd", "dbg", out=d_g, in_=g_d),
                    S.dma("gpsimd", "dbg", out=d_v, in_=v_d),
                    S.dma("gpsimd", "dbg", out=d_sg, in_=sg_d),
                    S.dma("gpsimd", "dbg", out=d_ol, in_=ol_d),
                    S.dma("gpsimd", "dbg", out=d_qg, in_=qg_d),
                    S.dma("gpsimd", "dbg", out=d_og, in_=og_d),
                    S.dma("gpsimd", "dbg", out=d_xn1, in_=xn1_d),
                    S.dma("gpsimd", "dbg", out=d_ccin, in_=cc_sin.ap()),
                    S.dma("gpsimd", "dbg", out=d_ccout, in_=cc_sout.ap()),
                ]
                S.wait("gpsimd", *finals)
            S.wait("sync", *finals)
            S.finalize(nc, block, stk)
            return nc
        ffn2 = emit_ffn(False, 2, out_final=True)
        S.wait("sync", *[ffn2[("out", t)] for t in range(NTT)])
        S.finalize(nc, block, stk)
    return nc


_prog_cache = {}


def _get_prog():
    if "nc" not in _prog_cache:
        _prog_cache["nc"] = build_program()
    return _prog_cache["nc"]


def kernel(L0_attn_x, L0_attn_Wq, L0_attn_Wk, L0_attn_Wv, L0_attn_Wg,
           L0_attn_gk_w1, L0_attn_gk_w2, L0_attn_gk_b2, L0_attn_g_norm_w,
           L0_attn_Wo, L0_ffn_clip_x, L0_ffn_clip_down, L0_ffn_w_gate,
           L0_ffn_w_up, L0_ffn_w_down):
    x = np.ascontiguousarray(np.asarray(L0_attn_x, dtype=np.float32))
    full = dict(
        Wq=np.ascontiguousarray(L0_attn_Wq, dtype=np.float32),
        Wk=np.ascontiguousarray(L0_attn_Wk, dtype=np.float32),
        Wv=np.ascontiguousarray(L0_attn_Wv, dtype=np.float32),
        Wg=np.ascontiguousarray(L0_attn_Wg, dtype=np.float32),
        gk1=np.ascontiguousarray(L0_attn_gk_w1, dtype=np.float32),
        gk2=np.ascontiguousarray(L0_attn_gk_w2, dtype=np.float32),
        gkb=np.ascontiguousarray(L0_attn_gk_b2, dtype=np.float32),
        gnw=np.ascontiguousarray(L0_attn_g_norm_w, dtype=np.float32),
        Wo=np.ascontiguousarray(L0_attn_Wo, dtype=np.float32),
        clipx=np.ascontiguousarray(L0_ffn_clip_x, dtype=np.float32),
        clipd=np.ascontiguousarray(L0_ffn_clip_down, dtype=np.float32),
        wga=np.ascontiguousarray(L0_ffn_w_gate, dtype=np.float32),
        wup=np.ascontiguousarray(L0_ffn_w_up, dtype=np.float32),
        wdo=np.ascontiguousarray(L0_ffn_w_down, dtype=np.float32),
    )
    in_maps = []
    for i in range(8):
        b, p = i // 4, i % 4
        m = dict(full)
        m["xs"] = np.ascontiguousarray(x[b, p * TOK:(p + 1) * TOK, :])
        gm = np.zeros(4, np.float32)
        gm[:p] = 1.0
        m["gmask"] = gm
        in_maps.append(m)
    nc = _get_prog()
    res = run_bass_kernel_spmd(nc, in_maps, core_ids=list(range(8)))
    outp = np.empty((2, 8192, D), np.float32)
    for i in range(8):
        b, p = i // 4, i % 4
        outp[b, p * TOK:(p + 1) * TOK, :] = res.results[i]["out"]
    return outp
